# version 03
# baseline (speedup 1.0000x reference)
import numpy as np
import ml_dtypes
import concourse.bass as bass
import concourse.mybir as mybir
from concourse.bass_utils import run_bass_kernel_spmd

F32 = mybir.dt.float32
BF16 = mybir.dt.bfloat16
I32 = mybir.dt.int32
I16 = mybir.dt.int16
U8 = mybir.dt.uint8
AF = mybir.ActivationFunctionType
ALU = mybir.AluOpType

DM = 2048
DIN = 2048
MLP = 16384
LSEQ = 4096
DTR = 128
KCONV = 4
PROD = 2048
OWN = 1024
SPAN = 4096
SEG = 512
NSEG = PROD // SEG
NCH = DM // 128
NMT = MLP // 128
EPS = 1e-5
GELU_C = 0.7978845608028654

CONST_VALS = [129.53125, -130.0, -4.0, -6.0, 0.0625, 0.125, -1.0, 1.0, 0.5,
              -0.5, 2.0, -2.0, -16.0, -8.0, 8.0, 64.0, 16128.0, 0.0, EPS,
              1.0 / DM, GELU_C, 0.044715]

DBG = False
SKIP_P0 = False

FLG_DEC = 0
FLG_XZ = 1
FLG_PRESCAN = 2
FLG_Y = 3
FLG_X1 = 4
NFLAG = 5


class NullIns:
    def then_inc(self, *a, **k):
        return self

    def __enter__(self):
        return self

    def __exit__(self, *a):
        return False


class NullEng:
    def __getattr__(self, name):
        return lambda *a, **k: NullIns()


class Sem:
    def __init__(self, handle):
        self.h = handle
        self.n = 0

    def inc(self, ins, k=1):
        ins.then_inc(self.h, k)
        self.n += k
        return self.n

    def dma(self, ins):
        return self.inc(ins, 16)


class E:

    ENGS = ("tensor", "vector", "scalar", "gpsimd", "sync")

    def __init__(self, nc, active):
        self.nc = nc
        self.active = active
        self._null = NullEng()

    def on(self, name):
        return getattr(self.nc, name) if name == self.active else self._null

    @property
    def T(self):
        return self.on("tensor")

    @property
    def V(self):
        return self.on("vector")

    @property
    def S(self):
        return self.on("scalar")

    @property
    def G(self):
        return self.on("gpsimd")

    @property
    def Y(self):
        return self.on("sync")

    def wait(self, names, sem, val):
        for nm in names:
            self.on(nm).wait_ge(sem.h, val)


def build_program(debug=DBG):
    nc = bass.Bass("TRN2", target_bir_lowering=False)

    def ein(name, shape, dt):
        return nc.dram_tensor(name, shape, dt, kind="ExternalInput").ap()

    x_in = ein("x_prod", [DM, PROD], F32)
    w_in_t = ein("w_in", [DM, 2 * DIN], F32)
    conv_wT = ein("conv_wT", [KCONV, DIN, DIN], F32)
    conv_b_r = ein("conv_b_r", [128, NCH], F32)
    w_x_t = ein("w_x", [DIN, DTR + 2 * 16], F32)
    w_dt_t = ein("w_dt", [DTR, DIN], F32)
    b_dt_r = ein("b_dt_r", [128, NCH], F32)
    a_log_r = ein("a_log_r", [128, NCH * 8], F32)
    d_r = ein("d_r", [128, NCH], F32)
    w_out_t = ein("w_out", [DIN, DM], F32)
    gate_pkT = ein("gate_pkT", [DM // 2, MLP], U8)
    up_pkT = ein("up_pkT", [DM // 2, MLP], U8)
    down_pkT = ein("down_pkT", [MLP // 2, DM], U8)
    gsT = ein("gsT", [DM // 32, MLP], F32)
    usT = ein("usT", [DM // 32, MLP], F32)
    dsT = ein("dsT", [MLP // 32, DM], F32)
    consts_in = ein("kconsts", [128, 32], F32)
    iota_in = ein("iota", [1, 64], I32)
    ones_col_in = ein("ones_col", [128, 1], BF16)
    ones_row_in = ein("ones_row", [1, 128], BF16)
    ones_f32_in = ein("ones_f32", [128, SEG], F32)
    role_in = ein("role", [1, 16], I32)

    out_t = nc.dram_tensor("out", [DM, OWN], F32, kind="ExternalOutput").ap()
    dbg_t = nc.dram_tensor("dbg", [DM, 20 * SEG], F32,
                           kind="ExternalOutput").ap() if debug else None

    def sh(name, shape, dt):
        return nc.dram_tensor(name, shape, dt, kind="Internal",
                              addr_space="Shared").ap()

    sh_wg = sh("sh_wg", [DM, MLP], BF16)
    sh_wu = sh("sh_wu", [DM, MLP], BF16)
    sh_wd = sh("sh_wd", [MLP, DM], BF16)
    sh_x1 = sh("sh_x1", [DM, SPAN], F32)
    sh_xc = sh("sh_xc", [DIN, SPAN], BF16)
    sh_r = sh("sh_r", [DIN, SPAN], BF16)
    sh_xs = sh("sh_xs", [DIN, SPAN], F32)
    sh_u = sh("sh_u", [DIN, SPAN], BF16)
    sh_delta = sh("sh_delta", [DIN, SPAN], F32)
    sh_bc = sh("sh_bc", [32, SPAN], F32)
    sh_y0 = sh("sh_y0", [DIN, SPAN], F32)
    sh_y1 = sh("sh_y1", [DIN, SPAN], F32)
    sh_flags = nc.dram_tensor("sh_flags", [2 * NFLAG, 16], I32, kind="Internal",
                              addr_space="Shared").ap()
    inter_sc = nc.dram_tensor("inter_sc", [MLP, PROD], BF16, kind="Internal").ap()

    ARENA_B = 149 * 1024

    import contextlib
    with contextlib.ExitStack() as _stk:
        def _e(cm):
            return _stk.enter_context(cm)
        arena = _e(nc.sbuf_tensor("arena", [128, ARENA_B], U8))
        s_consts = _e(nc.sbuf_tensor("s_consts", [128, 32], F32))
        s_role = _e(nc.sbuf_tensor("s_role", [1, 16], I32))
        s_iota = _e(nc.sbuf_tensor("s_iota", [1, 64], I32))
        s_onec = _e(nc.sbuf_tensor("s_onec", [128, 1], BF16))
        s_oner = _e(nc.sbuf_tensor("s_oner", [1, 128], BF16))
        s_onef = _e(nc.sbuf_tensor("s_onef", [128, SEG], F32))
        s_A = _e(nc.sbuf_tensor("s_A", [128, NCH * 8], F32))
        s_alog = _e(nc.sbuf_tensor("s_alog", [128, NCH * 8], F32))
        s_D = _e(nc.sbuf_tensor("s_D", [128, NCH], F32))
        s_cb = _e(nc.sbuf_tensor("s_cb", [128, NCH], F32))
        s_bdt = _e(nc.sbuf_tensor("s_bdt", [128, NCH], F32))
        s_state = _e(nc.sbuf_tensor("s_state", [128, NCH * 8], F32))
        s_rows = _e(nc.sbuf_tensor("s_rows", [1, 8 * SEG], F32))
        s_bcr = _e(nc.sbuf_tensor("s_bcr", [32, SEG], F32))
        s_wxw = _e(nc.sbuf_tensor("s_wxw", [128, NCH * 160], BF16))
        s_wdtw = _e(nc.sbuf_tensor("s_wdtw", [128, NCH * 128], BF16))
        ps = _e(nc.psum_tensor("ps", [128, 8, SEG], F32))
        qs_h = _e(nc.semaphore("qs"))
        qg_h = _e(nc.semaphore("qg"))
        mm_h = _e(nc.semaphore("mm"))
        ev_h = _e(nc.semaphore("ev"))
        es_h = _e(nc.semaphore("es"))
        ph_h = _e(nc.semaphore("ph"))
        fg_h = _e(nc.semaphore("fg"))
        blk = _e(nc.Block())
        
        for ci, v in enumerate(CONST_VALS):
            nc.const_aps.register(F32, float(v), s_consts[:, ci:ci + 1])

        def carve(off, shape, dt):
            esz = {F32: 4, BF16: 2, I32: 4, U8: 1, I16: 2}[dt]
            nb = int(np.prod(shape)) * esz
            assert off % 512 == 0 and off + nb <= ARENA_B, (off, shape, nb)
            v = arena.ap()[:, off:off + nb].bitcast(dt)
            if len(shape) == 1:
                return v
            names = " ".join(f"d{i}" for i in range(len(shape)))
            v = v.rearrange(f"p ({names}) -> p {names}",
                            **{f"d{i}": s for i, s in enumerate(shape)})
            return v

        def emit_body(e: E, sems):
            qs, qg, mm, ev, es, ph, fg = sems
            ALL = list(E.ENGS)

            aeng = e.on(e.active)
            par_reg = aeng.to_reg(aeng.value_load(s_role[0:1, 1:2]))
            isl_reg = aeng.to_reg(aeng.value_load(s_role[0:1, 2:3]))
            poll_c = aeng.alloc_register()
            poll_f = aeng.alloc_register()

            all_sems = sems

            def _snap():
                return [s.n for s in all_sems]

            def _restore(v):
                for s, x in zip(all_sems, v):
                    s.n = x

            def pp(fn, eng_name):
                en = e.on(eng_name)
                if eng_name != e.active:
                    fn(0)
                    return
                s0 = _snap()
                with en.If_eq(par_reg, 0):
                    fn(0)
                s1 = _snap()
                _restore(s0)
                with en.Else():
                    fn(1)
                assert _snap() == s1, "pp variants count differently"

            def rr(fn, eng_name):
                en = e.on(eng_name)
                if eng_name != e.active:
                    fn(0)
                    return
                s0 = _snap()
                with en.If_eq(isl_reg, 0):
                    with en.If_eq(par_reg, 0):
                        fn(0)
                    with en.Else():
                        _restore(s0)
                        fn(1)
                s1 = _snap()
                with en.Else():
                    with en.If_eq(par_reg, 0):
                        _restore(s0)
                        fn(2)
                    with en.Else():
                        _restore(s0)
                        fn(3)
                assert _snap() == s1, "rr variants count differently"

            def flag_set(fid, value):
                def go(p):
                    qg.dma(e.G.dma_start(
                        out=sh_flags[2 * fid + p:2 * fid + p + 1, 0:1],
                        in_=s_iota[0:1, value:value + 1]))
                pp(go, "gpsimd")

            def flag_poll(fid, value):
                def go(p):
                    g = e.G
                    row = 2 * fid + (1 - p)
                    def cond():
                        g.reg_load(poll_f, sh_flags[row:row + 1, 0:1])
                        g.reg_alu(poll_c, poll_f, value, ALU.is_lt)
                        return poll_c
                    with g.While(cond):
                        g.engine_nop()
                pp(go, "gpsimd")
                return fg.inc(e.G.engine_nop())

            def join():
                for en in ALL:
                    ph.inc(e.on(en).nop())
                t = ph.n
                for en in ALL:
                    e.on(en).wait_ge(ph.h, t)

            def dump(slot, src_ap):
                if not debug:
                    return
                qg.dma(e.G.dma_start(
                    out=dbg_t.rearrange("(c p) t -> p c t", p=128)
                    [:, :, slot * SEG:(slot + 1) * SEG], in_=src_ap))

            qs.dma(e.Y.dma_start(out=s_consts[:], in_=consts_in[:]))
            qs.dma(e.Y.dma_start(out=s_iota[:], in_=iota_in[:]))
            qs.dma(e.Y.dma_start(out=s_onec[:], in_=ones_col_in[:]))
            qs.dma(e.Y.dma_start(out=s_oner[:], in_=ones_row_in[:]))
            qs.dma(e.Y.dma_start(out=s_onef[:], in_=ones_f32_in[:]))
            qs.dma(e.Y.dma_start(out=s_D[:], in_=d_r[:]))
            qs.dma(e.Y.dma_start(out=s_cb[:], in_=conv_b_r[:]))
            qs.dma(e.Y.dma_start(out=s_bdt[:], in_=b_dt_r[:]))
            qs.dma(e.Y.dma_start(out=s_alog[:], in_=a_log_r[:]))
            e.wait(ALL, qs, qs.n)
            es.inc(e.S.activation(out=s_A[:], in_=s_alog[:], func=AF.Exp))
            e.wait(["vector"], es, es.n)
            ev.inc(e.V.tensor_scalar(out=s_A[:], in0=s_A[:], scalar1=-1.0,
                                     scalar2=None, op0=ALU.mult))
            e.wait(ALL, ev, ev.n)

            MC = 2048
            t_pk = carve(0, [MC], BF16)
            t_hi = carve(4 * 1024, [MC], BF16)
            t_lo = carve(8 * 1024, [MC], BF16)
            t_sg = carve(12 * 1024, [MC], BF16)
            t_rl = carve(16 * 1024, [MC], F32)
            t_i16 = carve(24 * 1024, [MC], I16)
            t_o1 = carve(28 * 1024, [MC], BF16)
            t_o2 = carve(32 * 1024, [MC], BF16)
            t_sc = carve(36 * 1024, [MC], F32)
            t_scb = carve(44 * 1024, [MC], BF16)

            def nib_decode(vin, vout):
                ev.inc(e.V.tensor_scalar(out=t_sg, in0=vin, scalar1=0.125,
                                         scalar2=129.53125, op0=ALU.mult, op1=ALU.add))
                ev.inc(e.V.tensor_scalar(out=t_sg, in0=t_sg, scalar1=1.0,
                                         scalar2=-130.0, op0=ALU.mult, op1=ALU.add))
                ev.inc(e.V.scalar_tensor_tensor(out=vout, in0=t_sg, scalar=-8.0,
                                                in1=vin, op0=ALU.mult, op1=ALU.add))
                ev.inc(e.V.tensor_scalar(out=t_i16, in0=vout, scalar1=64.0,
                                         scalar2=16128.0, op0=ALU.mult, op1=ALU.add))
                fb = t_i16.bitcast(BF16)
                e.wait(["scalar"], ev, ev.n)
                es.inc(e.S.activation(out=t_rl, in_=fb, func=AF.Relu,
                                      scale=-1.0, bias=1.0))
                e.wait(["vector"], es, es.n)
                ev.inc(e.V.tensor_sub(out=vout, in0=fb, in1=t_rl))
                ev.inc(e.V.tensor_scalar(out=t_sg, in0=t_sg, scalar1=-2.0,
                                         scalar2=1.0, op0=ALU.mult, op1=ALU.add))
                ev.inc(e.V.tensor_mul(out=vout, in0=vout, in1=t_sg))
                ev.inc(e.V.tensor_mul(out=vout, in0=vout, in1=t_scb))

            def decode_block(pk_ap, scT_ap, j0_src, jrows, mcols,
                             dst_fn_lo, dst_fn_hi):
                for jt in range(jrows // 128):
                    j0 = j0_src + jt * 128
                    for mt in range(mcols // MC):
                        m0 = mt * MC
                        e.wait(["gpsimd"], ev, ev.n)
                        e.wait(["sync"], ev, ev.n)
                        qg.dma(e.G.dma_start(out=t_pk,
                                             in_=pk_ap[j0:j0 + 128, m0:m0 + MC]))
                        sc_src = scT_ap[j0 // 16:j0 // 16 + 8, m0:m0 + MC]
                        sc_b = sc_src[:, None, :].broadcast_to((8, 16, MC))
                        qs.dma(e.Y.dma_start(out=t_sc, in_=sc_b))
                        e.wait(["vector"], qg, qg.n)
                        e.wait(["vector"], qs, qs.n)
                        ev.inc(e.V.tensor_copy(out=t_scb, in_=t_sc))
                        ev.inc(e.V.tensor_scalar(out=t_hi, in0=t_pk, scalar1=0.0625,
                                                 scalar2=129.53125, op0=ALU.mult,
                                                 op1=ALU.add))
                        ev.inc(e.V.tensor_scalar(out=t_hi, in0=t_hi, scalar1=1.0,
                                                 scalar2=-130.0, op0=ALU.mult,
                                                 op1=ALU.add))
                        ev.inc(e.V.scalar_tensor_tensor(out=t_lo, in0=t_hi,
                                                        scalar=-16.0, in1=t_pk,
                                                        op0=ALU.mult, op1=ALU.add))
                        nib_decode(t_lo, t_o1)
                        nib_decode(t_hi, t_o2)
                        e.wait(["sync"], ev, ev.n)
                        qs.dma(e.Y.dma_start(out=dst_fn_lo(jt, m0), in_=t_o1))
                        qs.dma(e.Y.dma_start(out=dst_fn_hi(jt, m0), in_=t_o2))
                        e.wait(["vector"], qs, qs.n)

            def p0(p):
                pk = gate_pkT if p == 0 else up_pkT
                sc = gsT if p == 0 else usT
                wdst = sh_wg if p == 0 else sh_wu
                decode_block(pk, sc, 0, DM // 2, MLP,
                             lambda jt, m0: wdst[jt * 128:jt * 128 + 128, m0:m0 + MC],
                             lambda jt, m0: wdst[DM // 2 + jt * 128:DM // 2 + jt * 128 + 128,
                                                 m0:m0 + MC])
                jb = p * (MLP // 4)
                decode_block(down_pkT, dsT, jb, MLP // 4, DM,
                             lambda jt, m0: sh_wd[jb + jt * 128:jb + jt * 128 + 128,
                                                  m0:m0 + MC],
                             lambda jt, m0: sh_wd[MLP // 2 + jb + jt * 128:
                                                  MLP // 2 + jb + jt * 128 + 128,
                                                  m0:m0 + MC])

            skip_p0 = SKIP_P0
            saved = (qs.n, qg.n, mm.n, ev.n, es.n, ph.n, fg.n)

            def run_p0(p):
                qs.n, qg.n, mm.n, ev.n, es.n, ph.n, fg.n = saved
                p0(p)

            if not skip_p0:
                if e.active in ("vector", "scalar", "gpsimd", "sync"):
                    en = e.on(e.active)
                    with en.If_eq(par_reg, 0):
                        run_p0(0)
                    with en.Else():
                        run_p0(1)
                else:
                    run_p0(0)

                e.wait(["gpsimd"], qs, qs.n)
                e.wait(["gpsimd"], qg, qg.n)
                flag_set(FLG_DEC, 1)
                n = flag_poll(FLG_DEC, 1)
                e.wait(ALL, fg, n)
            join()
            dump(8, sh_wg.rearrange("(c p) m -> p c m", p=128)[:, :, 0:SEG])
            dump(9, sh_wu.rearrange("(c p) m -> p c m", p=128)[:, :, 0:SEG])
            dump(10, sh_wd.rearrange("(c p) m -> p c m", p=128)[:, 0:16, 0:SEG])

            XN = carve(0, [NCH, PROD], BF16)
            XSEG = carve(64 * 1024, [NCH, SEG], F32)
            XBF = carve(96 * 1024, [NCH, SEG], BF16)
            SQBF = carve(112 * 1024, [NCH, SEG], BF16)

            def ln_seg(src_fn, seg, dst_xn_off, qsem, queue="sync"):
                e.wait([queue], ev, ev.n)
                e.wait(["scalar"], ev, ev.n)
                e.wait(["scalar"], mm, mm.n)
                pp(lambda p: qsem.dma(e.on(queue).dma_start(out=XSEG, in_=src_fn(p, seg))),
                   queue)
                e.wait(["scalar", "vector"], qsem, qsem.n)
                for c in range(NCH):
                    es.inc(e.S.activation(out=XBF[:, c, :], in_=XSEG[:, c, :],
                                          func=AF.Identity))
                    es.inc(e.S.activation(out=SQBF[:, c, :], in_=XSEG[:, c, :],
                                          func=AF.Square))
                e.wait(["tensor"], es, es.n)
                for c in range(NCH):
                    e.T.matmul(ps[0:1, 6, :], s_onec[:, 0:1], XBF[:, c, :],
                               start=(c == 0), stop=(c == NCH - 1))
                for c in range(NCH):
                    ins = e.T.matmul(ps[0:1, 7, :], s_onec[:, 0:1], SQBF[:, c, :],
                                     start=(c == 0), stop=(c == NCH - 1))
                mm.inc(ins)
                e.wait(["vector"], mm, mm.n)
                ev.inc(e.V.tensor_copy(out=s_rows[0:1, 0:SEG], in_=ps[0:1, 6, :]))
                ev.inc(e.V.scalar_tensor_tensor(out=s_rows[0:1, SEG:2 * SEG],
                                                in0=s_rows[0:1, 0:SEG], scalar=-1.0,
                                                in1=s_rows[0:1, 0:SEG],
                                                op0=ALU.mult, op1=ALU.mult))
                ev.inc(e.V.tensor_add(out=s_rows[0:1, SEG:2 * SEG],
                                      in0=s_rows[0:1, SEG:2 * SEG], in1=ps[0:1, 7, :]))
                e.wait(["scalar"], ev, ev.n)
                es.inc(e.S.activation(out=s_rows[0:1, 2 * SEG:3 * SEG],
                                      in_=s_rows[0:1, SEG:2 * SEG], func=AF.Sqrt,
                                      bias=EPS))
                e.wait(["vector"], es, es.n)
                ev.inc(e.V.reciprocal(out=s_rows[0:1, 2 * SEG:3 * SEG],
                                      in_=s_rows[0:1, 2 * SEG:3 * SEG]))
                ev.inc(e.V.scalar_tensor_tensor(out=s_rows[0:1, 3 * SEG:4 * SEG],
                                                in0=s_rows[0:1, 0:SEG], scalar=-1.0,
                                                in1=s_rows[0:1, 2 * SEG:3 * SEG],
                                                op0=ALU.mult, op1=ALU.mult))
                e.wait(["tensor"], ev, ev.n)
                e.wait(["scalar"], ev, ev.n)
                es.inc(e.S.activation(out=s_rows[0:1, 4 * SEG:5 * SEG].bitcast(BF16)[0:1, 0:SEG],
                                      in_=s_rows[0:1, 2 * SEG:3 * SEG], func=AF.Identity))
                es.inc(e.S.activation(out=s_rows[0:1, 4 * SEG:5 * SEG].bitcast(BF16)[0:1, SEG:2 * SEG],
                                      in_=s_rows[0:1, 3 * SEG:4 * SEG], func=AF.Identity))
                e.wait(["tensor"], es, es.n)
                rows_bf = s_rows[0:1, 4 * SEG:5 * SEG].bitcast(BF16)
                ins = e.T.matmul(ps[:, 6, :], s_oner[0:1, :], rows_bf[0:1, 0:SEG],
                                 start=True, stop=True)
                ins = e.T.matmul(ps[:, 7, :], s_oner[0:1, :], rows_bf[0:1, SEG:2 * SEG],
                                 start=True, stop=True)
                mm.inc(ins)
                e.wait(["vector"], mm, mm.n)
                for c in range(NCH):
                    ev.inc(e.V.tensor_mul(out=XBF[:, c, :], in0=XSEG[:, c, :],
                                          in1=ps[:, 6, :]))
                    ev.inc(e.V.tensor_add(out=XN[:, c, dst_xn_off:dst_xn_off + SEG],
                                          in0=XBF[:, c, :], in1=ps[:, 7, :]))
                ev.inc(e.V.engine_nop())

            for seg in range(NSEG):
                ln_seg(lambda p, s: x_in.rearrange("(c p) t -> p c t", p=128)
                       [:, :, s * SEG:(s + 1) * SEG], seg, seg * SEG, qs)
                if seg == 0 and debug:
                    e.wait(["gpsimd"], ev, ev.n)
                    dump(0, XN[:, :, 0:SEG])
                    e.wait(["vector"], qg, qg.n)
            join()

            WG = carve(128 * 1024, [2, NCH, 128], BF16)
            WU = carve(136 * 1024, [2, NCH, 128], BF16)
            GEL2 = carve(144 * 1024, [SEG], F32)
            INT3 = carve(148 * 1024, [SEG], BF16)

            def ffn_gate_up2(xn_ap, tspan, inter_cols0):
                nhalf = tspan // 1024
                mark_mm = {}
                it = 0
                for hf in range(nhalf):
                    t0 = hf * 1024
                    for mt in range(NMT):
                        b = mt % 2
                        if it >= 2:
                            e.wait(["sync"], mm, mark_mm[it - 2])
                        qs.dma(e.Y.dma_start(
                            out=WG[:, b, :, :],
                            in_=sh_wg.rearrange("(c p) m -> p c m", p=128)
                            [:, :, mt * 128:(mt + 1) * 128]))
                        qs.dma(e.Y.dma_start(
                            out=WU[:, b, :, :],
                            in_=sh_wu.rearrange("(c p) m -> p c m", p=128)
                            [:, :, mt * 128:(mt + 1) * 128]))
                        e.wait(["tensor"], qs, qs.n)
                        if it >= 2:
                            e.wait(["tensor"], ev, ev_mark[it - 2])
                        for sg in range(2):
                            for k in range(NCH):
                                e.T.matmul(ps[:, 4 * b + sg, :], WG[:, b, k, :],
                                           xn_ap[:, k, t0 + sg * SEG:t0 + (sg + 1) * SEG],
                                           start=(k == 0), stop=(k == NCH - 1))
                        for sg in range(2):
                            for k in range(NCH):
                                ins = e.T.matmul(ps[:, 4 * b + 2 + sg, :], WU[:, b, k, :],
                                                 xn_ap[:, k, t0 + sg * SEG:t0 + (sg + 1) * SEG],
                                                 start=(k == 0), stop=(k == NCH - 1))
                        mm.inc(ins)
                        mark_mm[it] = mm.n
                        e.wait(["scalar"], mm, mm.n)
                        for sg in range(2):
                            e.wait(["scalar"], ev, ev.n)
                            es.inc(e.S.activation(out=GEL2, in_=ps[:, 4 * b + sg, :],
                                                  func=AF.Gelu_apprx_tanh))
                            e.wait(["vector"], es, es.n)
                            ev.inc(e.V.tensor_mul(out=INT3, in0=GEL2,
                                                  in1=ps[:, 4 * b + 2 + sg, :]))
                            e.wait(["sync"], ev, ev.n)
                            qs.dma(e.Y.dma_start(
                                out=inter_sc[mt * 128:(mt + 1) * 128,
                                             inter_cols0 + t0 + sg * SEG:
                                             inter_cols0 + t0 + (sg + 1) * SEG],
                                in_=INT3))
                            e.wait(["vector"], qs, qs.n)
                        ev_mark[it] = ev.n
                        it += 1

            ev_mark = {}
            ffn_gate_up2(XN, PROD, 0)
            join()
            for kb in range(8):
                dump(12 + kb, inter_sc.rearrange("(c p) m -> p c m", p=128)
                     [:, 16 * kb:16 * (kb + 1), 0:SEG])

            IBLK = carve(96 * 1024, [2, 16, SEG], BF16)
            WD = carve(128 * 1024, [2, 16, 128], BF16)

            def ffn_down(inter_cols0, tspan, x_src_fn, x1_dst_fn, extra_evac=None,
                         out_scale_store=None):
                nseg = tspan // SEG
                for seg in range(nseg):
                    e.wait(["sync"], qs, qs.n)
                    e.wait(["sync"], ev, ev.n)
                    pp(lambda p: qs.dma(e.Y.dma_start(out=XSEG, in_=x_src_fn(p, seg))),
                       "sync")
                    for hb in range(16):
                        bank = hb % 2
                        if hb >= 2:
                            e.wait(["tensor"], ev, ev.n)
                        for ktb in range(8):
                            bb = ktb % 2
                            e.wait(["sync"], mm, mm.n)
                            qs.dma(e.Y.dma_start(
                                out=IBLK[:, bb, :, :],
                                in_=inter_sc.rearrange("(kb g p) t -> p kb g t", p=128, g=16)
                                [:, ktb, :, inter_cols0 + seg * SEG:
                                 inter_cols0 + (seg + 1) * SEG]))
                            qs.dma(e.Y.dma_start(
                                out=WD[:, bb, :, :],
                                in_=sh_wd.rearrange("(kb g p) h -> p kb g h", p=128, g=16)
                                [:, ktb, :, hb * 128:(hb + 1) * 128]))
                            e.wait(["tensor"], qs, qs.n)
                            for g in range(16):
                                ins = e.T.matmul(ps[:, bank, :], WD[:, bb, g, :],
                                                 IBLK[:, bb, g, :],
                                                 start=(ktb == 0 and g == 0),
                                                 stop=(ktb == 7 and g == 15))
                            mm.inc(ins)
                        e.wait(["vector"], mm, mm.n)
                        e.wait(["vector"], qs, qs.n)
                        ev.inc(e.V.scalar_tensor_tensor(
                            out=XSEG[:, hb, :], in0=ps[:, bank, :], scalar=0.5,
                            in1=XSEG[:, hb, :], op0=ALU.mult, op1=ALU.add))
                    ev.inc(e.V.engine_nop())
                    e.wait(["sync"], ev, ev.n)
                    pp(lambda p: qs.dma(e.Y.dma_start(out=x1_dst_fn(p, seg), in_=XSEG)),
                       "sync")
                    if extra_evac is not None:
                        extra_evac(seg)
                    e.wait(["vector"], qs, qs.n)

            def x_src(p, seg):
                return x_in.rearrange("(c p) t -> p c t", p=128)[:, :, seg * SEG:(seg + 1) * SEG]

            def x1_dst(p, seg):
                c0 = p * PROD + seg * SEG
                return sh_x1.rearrange("(c p) t -> p c t", p=128)[:, :, c0:c0 + SEG]

            ffn_down(0, PROD, x_src, x1_dst)
            e.wait(["gpsimd"], qs, qs.n)
            flag_set(FLG_X1, 1)
            join()
            dump(1, sh_x1.rearrange("(c p) t -> p c t", p=128)[:, :, 0:SEG])

            def x1_src(p, seg):
                c0 = p * PROD + seg * SEG
                return sh_x1.rearrange("(c p) t -> p c t", p=128)[:, :, c0:c0 + SEG]

            for seg in range(NSEG):
                ln_seg(x1_src, seg, seg * SEG, qs)
            join()

            WI = carve(128 * 1024, [2, NCH, 128], BF16)
            EVA = carve(136 * 1024, [SEG], BF16)

            for ob in range(32):
                b = ob % 2
                if ob >= 2:
                    e.wait(["gpsimd"], mm, mm.n)
                qg.dma(e.G.dma_start(
                    out=WI[:, b, :, :],
                    in_=w_in_t.rearrange("(c p) m -> p c m", p=128)
                    [:, :, ob * 128:(ob + 1) * 128]))
                e.wait(["tensor"], qg, qg.n)
                for seg in range(NSEG):
                    if not (ob == 0 and seg == 0):
                        e.wait(["tensor"], es, es.n)
                    for k in range(NCH):
                        ins = e.T.matmul(ps[:, 6 + (seg % 2), :], WI[:, b, k, :],
                                         XN[:, k, seg * SEG:(seg + 1) * SEG],
                                         start=(k == 0), stop=(k == NCH - 1))
                    mm.inc(ins)
                    e.wait(["scalar"], mm, mm.n)
                    es.inc(e.S.activation(out=EVA, in_=ps[:, 6 + (seg % 2), :],
                                          func=AF.Identity))
                    e.wait(["sync"], es, es.n)
                    if ob < NCH:
                        dst = sh_xc
                        rw = ob
                    else:
                        dst = sh_r
                        rw = ob - NCH

                    def store(p, dst=dst, rw=rw, seg=seg):
                        c0 = p * PROD + seg * SEG
                        qs.dma(e.Y.dma_start(
                            out=dst[rw * 128:(rw + 1) * 128, c0:c0 + SEG],
                            in_=EVA))
                    pp(lambda p: store(p), "sync")
                    e.wait(["scalar"], qs, qs.n)
            e.wait(["gpsimd"], qs, qs.n)
            flag_set(FLG_XZ, 1)
            n = flag_poll(FLG_XZ, 1)
            e.wait(ALL, fg, n)
            join()

            UFULL = carve(64 * 1024, [NCH, PROD], BF16)
            XCW = carve(0, [NCH, SEG + 4], BF16)
            CW = carve(17 * 1024, [2, NCH, KCONV, 128], BF16)
            UEV = carve(50 * 1024, [SEG], BF16)
            XSEV = carve(51 * 1024, [SEG], F32)

            for ob in range(NCH):
                b = ob % 2
                if ob >= 2:
                    e.wait(["gpsimd"], mm, mm.n)
                for kk in range(KCONV):
                    qg.dma(e.G.dma_start(
                        out=CW[:, b, :, kk, :],
                        in_=conv_wT.rearrange("k (c p) o -> p k c o", p=128)
                        [:, kk, :, ob * 128:(ob + 1) * 128]))
                for seg in range(NSEG):
                    def ldwin(p, seg=seg):
                        c0 = p * PROD + seg * SEG
                        if c0 == 0:
                            qs.dma(e.Y.dma_start(
                                out=XCW[:, :, 4:SEG + 4],
                                in_=sh_xc.rearrange("(c p) t -> p c t", p=128)
                                [:, :, 0:SEG]))
                        else:
                            qs.dma(e.Y.dma_start(
                                out=XCW[:, :, 0:SEG + 4],
                                in_=sh_xc.rearrange("(c p) t -> p c t", p=128)
                                [:, :, c0 - 4:c0 + SEG]))
                    e.wait(["vector"], mm, mm.n)
                    e.wait(["sync"], mm, mm.n)
                    ev.inc(e.V.memset(XCW[:, :, 0:4], 0.0))
                    e.wait(["sync"], ev, ev.n)
                    pp(ldwin, "sync")
                    e.wait(["tensor"], qs, qs.n)
                    e.wait(["tensor"], qg, qg.n)
                    if not (ob == 0 and seg == 0):
                        e.wait(["tensor"], es, es.n)
                    for c in range(NCH):
                        for k in range(KCONV):
                            ins = e.T.matmul(ps[:, 6 + (seg % 2), :],
                                             CW[:, b, c, k, :],
                                             XCW[:, c, 1 + k:1 + k + SEG],
                                             start=(c == 0 and k == 0),
                                             stop=(c == NCH - 1 and k == KCONV - 1))
                    mm.inc(ins)
                    e.wait(["scalar"], mm, mm.n)
                    es.inc(e.S.activation(out=UEV, in_=ps[:, 6 + (seg % 2), :],
                                          func=AF.Silu, bias=s_cb[:, ob:ob + 1]))
                    es.inc(e.S.activation(out=XSEV, in_=ps[:, 6 + (seg % 2), :],
                                          func=AF.Identity, bias=s_cb[:, ob:ob + 1]))
                    e.wait(["vector"], es, es.n)
                    ev.inc(e.V.tensor_copy(out=UFULL[:, ob, seg * SEG:(seg + 1) * SEG],
                                           in_=UEV))
                    e.wait(["sync"], es, es.n)

                    def store_u(p, ob=ob, seg=seg):
                        c0 = p * PROD + seg * SEG
                        qs.dma(e.Y.dma_start(out=sh_u[ob * 128:(ob + 1) * 128, c0:c0 + SEG],
                                             in_=UEV))
                        qs.dma(e.Y.dma_start(out=sh_xs[ob * 128:(ob + 1) * 128, c0:c0 + SEG],
                                             in_=XSEV))
                    pp(store_u, "sync")
                    e.wait(["scalar"], qs, qs.n)
            join()

            qg.dma(e.G.dma_start(out=s_wxw.rearrange("p (c f) -> p c f", c=NCH),
                                 in_=w_x_t.rearrange("(c p) f -> p c f", p=128)))
            qg.dma(e.G.dma_start(out=s_wdtw.rearrange("p (c f) -> p c f", c=NCH),
                                 in_=w_dt_t.rearrange("p (c f) -> p c f", c=NCH)))
            e.wait(["tensor"], qg, qg.n)
            DBC = carve(0, [SEG], BF16)
            DEL = carve(2 * 1024, [SEG], F32)
            DE2 = carve(6 * 1024, [SEG], F32)
            DE3 = carve(10 * 1024, [SEG], F32)

            for seg in range(NSEG):
                if seg > 0:
                    e.wait(["tensor"], ev, ev.n)
                    e.wait(["tensor"], es, es.n)
                    e.wait(["scalar"], mm, mm.n)
                for c in range(NCH):
                    e.T.matmul(ps[:, 6, :], s_wxw.rearrange("p (c f) -> p c f", c=NCH)
                               [:, c, 0:DTR],
                               UFULL[:, c, seg * SEG:(seg + 1) * SEG],
                               start=(c == 0), stop=(c == NCH - 1))
                for c in range(NCH):
                    ins = e.T.matmul(ps[0:32, 7, :], s_wxw.rearrange("p (c f) -> p c f", c=NCH)
                                     [:, c, DTR:DTR + 32],
                                     UFULL[:, c, seg * SEG:(seg + 1) * SEG],
                                     start=(c == 0), stop=(c == NCH - 1))
                mm.inc(ins)
                e.wait(["scalar", "vector"], mm, mm.n)
                es.inc(e.S.activation(out=DBC, in_=ps[:, 6, :], func=AF.Identity))
                ev.inc(e.V.tensor_copy(out=s_bcr[:, :], in_=ps[0:32, 7, :]))
                e.wait(["sync"], ev, ev.n)

                def store_bc(p, seg=seg):
                    c0 = p * PROD + seg * SEG
                    qs.dma(e.Y.dma_start(out=sh_bc[:, c0:c0 + SEG], in_=s_bcr[:, :]))
                pp(store_bc, "sync")
                e.wait(["tensor"], es, es.n)
                e.wait(["tensor"], ev, ev.n)
                for ob in range(NCH):
                    if ob > 0:
                        e.wait(["tensor"], es, es.n)
                    ins = e.T.matmul(ps[:, 7, :],
                                     s_wdtw.rearrange("p (c f) -> p c f", c=NCH)[:, ob, :],
                                     DBC, start=True, stop=True)
                    mm.inc(ins)
                    e.wait(["scalar"], mm, mm.n)
                    es.inc(e.S.activation(out=DEL, in_=ps[:, 7, :], func=AF.Identity,
                                          bias=s_bdt[:, ob:ob + 1]))
                    e.wait(["vector"], es, es.n)
                    ev.inc(e.V.tensor_relu(out=DE2, in_=DEL))
                    e.wait(["scalar"], ev, ev.n)
                    es.inc(e.S.activation(out=DE3, in_=DEL, func=AF.Abs))
                    e.wait(["scalar"], es, es.n)
                    es.inc(e.S.activation(out=DE3, in_=DE3, func=AF.Exp, scale=-1.0))
                    es.inc(e.S.activation(out=DE3, in_=DE3, func=AF.Ln, bias=1.0))
                    e.wait(["vector"], es, es.n)
                    ev.inc(e.V.tensor_add(out=DEL, in0=DE2, in1=DE3))
                    e.wait(["sync"], ev, ev.n)

                    def store_del(p, ob=ob, seg=seg):
                        c0 = p * PROD + seg * SEG
                        qs.dma(e.Y.dma_start(
                            out=sh_delta[ob * 128:(ob + 1) * 128, c0:c0 + SEG],
                            in_=DEL))
                    pp(store_del, "sync")
                    e.wait(["scalar", "tensor"], qs, qs.n)
            e.wait(["gpsimd"], qs, qs.n)
            flag_set(FLG_PRESCAN, 1)
            n = flag_poll(FLG_PRESCAN, 1)
            e.wait(ALL, fg, n)
            join()
            dump(2, sh_xc.rearrange("(c p) t -> p c t", p=128)[:, :, 0:SEG])
            dump(3, sh_u.rearrange("(c p) t -> p c t", p=128)[:, :, 0:SEG])
            dump(4, sh_delta.rearrange("(c p) t -> p c t", p=128)[:, :, 0:SEG])

            DELC = carve(0, [NCH, SEG], F32)
            UC = carve(32 * 1024, [NCH, SEG], BF16)
            DUC = carve(48 * 1024, [NCH, SEG], F32)
            YC = carve(80 * 1024, [NCH, SEG], F32)
            BCB = carve(112 * 1024, [2, SEG], F32)
            DAT = carve(116 * 1024, [SEG], F32)
            DBT = carve(118 * 1024, [SEG], F32)
            HT = carve(120 * 1024, [SEG], F32)
            BCRB = carve(122 * 1024, [2, SEG], BF16)

            ev.inc(e.V.memset(s_state[:, :], 0.0))
            e.wait(ALL, ev, ev.n)

            for tc in range(SPAN // SEG):
                c0 = tc * SEG
                e.wait(["sync"], ev, ev.n)
                e.wait(["sync"], es, es.n)
                qs.dma(e.Y.dma_start(out=DELC,
                                     in_=sh_delta.rearrange("(c p) t -> p c t", p=128)
                                     [:, :, c0:c0 + SEG]))
                qs.dma(e.Y.dma_start(out=UC,
                                     in_=sh_u.rearrange("(c p) t -> p c t", p=128)
                                     [:, :, c0:c0 + SEG]))
                e.wait(["vector", "scalar"], qs, qs.n)
                for c in range(NCH):
                    ev.inc(e.V.tensor_mul(out=DUC[:, c, :], in0=DELC[:, c, :],
                                          in1=UC[:, c, :]))
                    ev.inc(e.V.tensor_scalar(out=YC[:, c, :], in0=UC[:, c, :],
                                             scalar1=s_D[:, c:c + 1], scalar2=None,
                                             op0=ALU.mult))
                for nn in range(8):
                    def ld_rows(p, nn=nn, c0=c0):
                        ng = p * 8 + nn
                        qg.dma(e.G.dma_start(out=BCRB[0:1, 0, :],
                                             in_=sh_bc[ng:ng + 1, c0:c0 + SEG]))
                        qg.dma(e.G.dma_start(out=BCRB[0:1, 1, :],
                                             in_=sh_bc[16 + ng:17 + ng, c0:c0 + SEG]))
                    e.wait(["gpsimd"], mm, mm.n)
                    pp(ld_rows, "gpsimd")
                    e.wait(["tensor"], qg, qg.n)
                    e.wait(["tensor"], ev, ev.n)
                    e.T.matmul(ps[:, 6, :], s_oner[0:1, :], BCRB[0:1, 0, :],
                               start=True, stop=True)
                    ins = e.T.matmul(ps[:, 7, :], s_oner[0:1, :], BCRB[0:1, 1, :],
                                     start=True, stop=True)
                    mm.inc(ins)
                    e.wait(["vector"], mm, mm.n)
                    ev.inc(e.V.tensor_copy(out=BCB[:, 0, :], in_=ps[:, 6, :]))
                    ev.inc(e.V.tensor_copy(out=BCB[:, 1, :], in_=ps[:, 7, :]))
                    for c in range(NCH):
                        col = c * 8 + nn
                        es.inc(e.S.activation(out=DAT, in_=DELC[:, c, :], func=AF.Exp,
                                              scale=s_A[:, col:col + 1]))
                        ev.inc(e.V.tensor_mul(out=DBT, in0=DUC[:, c, :], in1=BCB[:, 0, :]))
                        e.wait(["vector"], es, es.n)
                        ev.inc(e.V.tensor_tensor_scan(out=HT, data0=DAT, data1=DBT,
                                                      initial=s_state[:, col:col + 1],
                                                      op0=ALU.mult, op1=ALU.add))
                        ev.inc(e.V.tensor_copy(out=s_state[:, col:col + 1],
                                               in_=HT[:, SEG - 1:SEG]))
                        ev.inc(e.V.tensor_mul(out=DBT, in0=HT, in1=BCB[:, 1, :]))
                        ev.inc(e.V.tensor_add(out=YC[:, c, :], in0=YC[:, c, :], in1=DBT))
                        e.wait(["scalar"], ev, ev.n)
                ev.inc(e.V.engine_nop())
                e.wait(["sync"], ev, ev.n)

                def store_y(p, tc=tc):
                    dst = sh_y0 if p == 0 else sh_y1
                    qs.dma(e.Y.dma_start(
                        out=dst.rearrange("(c p) t -> p c t", p=128)
                        [:, :, tc * SEG:(tc + 1) * SEG], in_=YC))
                pp(store_y, "sync")
                e.wait(["gpsimd"], qs, qs.n)
                flag_set(FLG_Y, tc + 1)
                e.wait(["vector"], qs, qs.n)
            n = flag_poll(FLG_Y, SPAN // SEG)
            e.wait(ALL, fg, n)
            nx1 = flag_poll(FLG_X1, 1)
            e.wait(ALL, fg, nx1)
            join()
            dump(5, sh_y0.rearrange("(c p) t -> p c t", p=128)[:, :, 0:SEG])
            dump(6, sh_y1.rearrange("(c p) t -> p c t", p=128)[:, :, 0:SEG])

            YM = carve(0, [NCH, SEG], F32)
            YP = carve(32 * 1024, [NCH, SEG], F32)
            GAT = carve(64 * 1024, [2, NCH, SEG], BF16)

            def own_off(role):
                return 1024 * role

            for sego in range(2):
                e.wait(["sync"], ev, ev.n)
                e.wait(["sync"], es, es.n)
                def ld_own(role, src, dst, cast=False, sego=sego):
                    c0 = own_off(role) + sego * SEG
                    qs.dma(e.Y.dma_start(out=dst,
                                         in_=src.rearrange("(c p) t -> p c t", p=128)
                                         [:, :, c0:c0 + SEG]))
                rr(lambda ro: ld_own(ro, sh_y0, YM), "sync")
                rr(lambda ro: ld_own(ro, sh_y1, YP), "sync")
                e.wait(["vector"], qs, qs.n)
                for c in range(NCH):
                    ev.inc(e.V.tensor_add(out=YM[:, c, :], in0=YM[:, c, :],
                                          in1=YP[:, c, :]))
                e.wait(["sync"], ev, ev.n)
                RBv = carve(32 * 1024, [NCH, SEG], BF16)
                rr(lambda ro: ld_own(ro, sh_r, RBv), "sync")
                e.wait(["scalar"], qs, qs.n)
                for c in range(NCH):
                    es.inc(e.S.activation(out=RBv[:, c, :], in_=RBv[:, c, :],
                                          func=AF.Silu))
                e.wait(["vector"], es, es.n)
                for c in range(NCH):
                    ev.inc(e.V.tensor_mul(out=GAT[:, sego, c, :], in0=YM[:, c, :],
                                          in1=RBv[:, c, :]))
                ev.inc(e.V.engine_nop())
            e.wait(["tensor"], ev, ev.n)

            XSOWN = carve(0, [NCH, SEG], F32)
            X1OWN = carve(32 * 1024, [NCH, SEG], F32)
            XN2OWN = carve(96 * 1024, [NCH, 2 * SEG], BF16)
            WO2 = carve(128 * 1024, [2, NCH, 128], BF16)

            for sego in range(2):
                def ld_own2(role, src, dst, sego=sego):
                    c0 = own_off(role) + sego * SEG
                    qs.dma(e.Y.dma_start(out=dst,
                                         in_=src.rearrange("(c p) t -> p c t", p=128)
                                         [:, :, c0:c0 + SEG]))
                e.wait(["sync"], ev, ev.n)
                rr(lambda ro: ld_own2(ro, sh_xs, XSOWN), "sync")
                rr(lambda ro: ld_own2(ro, sh_x1, X1OWN), "sync")
                e.wait(["vector"], qs, qs.n)
                for ob in range(NCH):
                    b = ob % 2
                    if ob >= 2:
                        e.wait(["gpsimd"], mm, mm.n)
                    qg.dma(e.G.dma_start(
                        out=WO2[:, b, :, :],
                        in_=w_out_t.rearrange("(c p) m -> p c m", p=128)
                        [:, :, ob * 128:(ob + 1) * 128]))
                    e.wait(["tensor"], qg, qg.n)
                    if not (sego == 0 and ob == 0):
                        e.wait(["tensor"], ev, ev.n)
                    for c in range(NCH):
                        ins = e.T.matmul(ps[:, 6 + b, :], WO2[:, b, c, :],
                                         GAT[:, sego, c, :],
                                         start=(c == 0), stop=(c == NCH - 1))
                    mm.inc(ins)
                    e.wait(["vector"], mm, mm.n)
                    ev.inc(e.V.tensor_add(out=X1OWN[:, ob, :], in0=X1OWN[:, ob, :],
                                          in1=ps[:, 6 + b, :]))
                    ev.inc(e.V.tensor_add(out=X1OWN[:, ob, :], in0=X1OWN[:, ob, :],
                                          in1=XSOWN[:, ob, :]))
                ev.inc(e.V.engine_nop())
                e.wait(["sync"], ev, ev.n)

                def st_x2(role, sego=sego):
                    c0 = own_off(role) + sego * SEG
                    qs.dma(e.Y.dma_start(
                        out=sh_x1.rearrange("(c p) t -> p c t", p=128)[:, :, c0:c0 + SEG],
                        in_=X1OWN))
                rr(st_x2, "sync")
                e.wait(["vector"], qs, qs.n)
            join()
            dump(7, sh_x1.rearrange("(c p) t -> p c t", p=128)[:, :, 0:SEG])

            def x2_src(role, seg):
                c0 = own_off(role) + seg * SEG
                return sh_x1.rearrange("(c p) t -> p c t", p=128)[:, :, c0:c0 + SEG]

            def ln_seg_own(seg):
                e.wait(["sync"], ev, ev.n)
                e.wait(["sync"], mm, mm.n)
                e.wait(["scalar"], ev, ev.n)
                e.wait(["scalar"], mm, mm.n)
                rr(lambda ro: qs.dma(e.Y.dma_start(out=X1OWN, in_=x2_src(ro, seg))),
                   "sync")
                e.wait(["scalar", "vector"], qs, qs.n)
                for c in range(NCH):
                    es.inc(e.S.activation(out=XN2OWN[:, c, seg * SEG:(seg + 1) * SEG],
                                          in_=X1OWN[:, c, :], func=AF.Identity))
                    es.inc(e.S.activation(out=GAT[:, 0, c, :], in_=X1OWN[:, c, :],
                                          func=AF.Square))
                e.wait(["tensor"], es, es.n)
                for c in range(NCH):
                    e.T.matmul(ps[0:1, 6, :], s_onec[:, 0:1],
                               XN2OWN[:, c, seg * SEG:(seg + 1) * SEG],
                               start=(c == 0), stop=(c == NCH - 1))
                for c in range(NCH):
                    ins = e.T.matmul(ps[0:1, 7, :], s_onec[:, 0:1], GAT[:, 0, c, :],
                                     start=(c == 0), stop=(c == NCH - 1))
                mm.inc(ins)
                e.wait(["vector"], mm, mm.n)
                ev.inc(e.V.tensor_copy(out=s_rows[0:1, 0:SEG], in_=ps[0:1, 6, :]))
                ev.inc(e.V.scalar_tensor_tensor(out=s_rows[0:1, SEG:2 * SEG],
                                                in0=s_rows[0:1, 0:SEG], scalar=-1.0,
                                                in1=s_rows[0:1, 0:SEG],
                                                op0=ALU.mult, op1=ALU.mult))
                ev.inc(e.V.tensor_add(out=s_rows[0:1, SEG:2 * SEG],
                                      in0=s_rows[0:1, SEG:2 * SEG], in1=ps[0:1, 7, :]))
                e.wait(["scalar"], ev, ev.n)
                es.inc(e.S.activation(out=s_rows[0:1, 2 * SEG:3 * SEG],
                                      in_=s_rows[0:1, SEG:2 * SEG], func=AF.Sqrt,
                                      bias=EPS))
                e.wait(["vector"], es, es.n)
                ev.inc(e.V.reciprocal(out=s_rows[0:1, 2 * SEG:3 * SEG],
                                      in_=s_rows[0:1, 2 * SEG:3 * SEG]))
                ev.inc(e.V.scalar_tensor_tensor(out=s_rows[0:1, 3 * SEG:4 * SEG],
                                                in0=s_rows[0:1, 0:SEG], scalar=-1.0,
                                                in1=s_rows[0:1, 2 * SEG:3 * SEG],
                                                op0=ALU.mult, op1=ALU.mult))
                e.wait(["scalar"], ev, ev.n)
                es.inc(e.S.activation(out=s_rows[0:1, 4 * SEG:5 * SEG].bitcast(BF16)
                                      [0:1, 0:SEG],
                                      in_=s_rows[0:1, 2 * SEG:3 * SEG], func=AF.Identity))
                es.inc(e.S.activation(out=s_rows[0:1, 4 * SEG:5 * SEG].bitcast(BF16)
                                      [0:1, SEG:2 * SEG],
                                      in_=s_rows[0:1, 3 * SEG:4 * SEG], func=AF.Identity))
                e.wait(["tensor"], es, es.n)
                rows_bf = s_rows[0:1, 4 * SEG:5 * SEG].bitcast(BF16)
                e.T.matmul(ps[:, 6, :], s_oner[0:1, :], rows_bf[0:1, 0:SEG],
                           start=True, stop=True)
                ins = e.T.matmul(ps[:, 7, :], s_oner[0:1, :], rows_bf[0:1, SEG:2 * SEG],
                                 start=True, stop=True)
                mm.inc(ins)
                e.wait(["vector"], mm, mm.n)
                for c in range(NCH):
                    ev.inc(e.V.tensor_mul(out=GAT[:, 1, c, :], in0=X1OWN[:, c, :],
                                          in1=ps[:, 6, :]))
                    ev.inc(e.V.tensor_add(out=XN2OWN[:, c, seg * SEG:(seg + 1) * SEG],
                                          in0=GAT[:, 1, c, :], in1=ps[:, 7, :]))
                ev.inc(e.V.engine_nop())
                e.wait(["tensor"], ev, ev.n)

            for sego in range(2):
                ln_seg_own(sego)
            join()

            WG2 = carve(128 * 1024, [2, NCH, 128], BF16)
            WU2 = carve(136 * 1024, [2, NCH, 128], BF16)
            GEL5 = carve(144 * 1024, [2, SEG], F32)
            INT5 = carve(148 * 1024, [SEG], BF16)
            ev_mark5 = {}
            mark_mm5 = {}
            for mt in range(NMT):
                b = mt % 2
                if mt >= 2:
                    e.wait(["sync"], mm, mark_mm5[mt - 2])
                qs.dma(e.Y.dma_start(
                    out=WG2[:, b, :, :],
                    in_=sh_wg.rearrange("(c p) m -> p c m", p=128)
                    [:, :, mt * 128:(mt + 1) * 128]))
                qs.dma(e.Y.dma_start(
                    out=WU2[:, b, :, :],
                    in_=sh_wu.rearrange("(c p) m -> p c m", p=128)
                    [:, :, mt * 128:(mt + 1) * 128]))
                e.wait(["tensor"], qs, qs.n)
                if mt >= 2:
                    e.wait(["tensor"], ev, ev_mark5[mt - 2])
                for sg in range(2):
                    for k in range(NCH):
                        e.T.matmul(ps[:, 4 * b + sg, :], WG2[:, b, k, :],
                                   XN2OWN[:, k, sg * SEG:(sg + 1) * SEG],
                                   start=(k == 0), stop=(k == NCH - 1))
                for sg in range(2):
                    for k in range(NCH):
                        ins = e.T.matmul(ps[:, 4 * b + 2 + sg, :], WU2[:, b, k, :],
                                         XN2OWN[:, k, sg * SEG:(sg + 1) * SEG],
                                         start=(k == 0), stop=(k == NCH - 1))
                mm.inc(ins)
                mark_mm5[mt] = mm.n
                e.wait(["scalar"], mm, mm.n)
                for sg in range(2):
                    e.wait(["scalar"], ev, ev.n)
                    es.inc(e.S.activation(out=GEL5[:, sg, :],
                                          in_=ps[:, 4 * b + sg, :],
                                          func=AF.Gelu_apprx_tanh))
                    e.wait(["vector"], es, es.n)
                    ev.inc(e.V.tensor_mul(out=INT5,
                                          in0=GEL5[:, sg, :],
                                          in1=ps[:, 4 * b + 2 + sg, :]))
                    e.wait(["sync"], ev, ev.n)
                    qs.dma(e.Y.dma_start(
                        out=inter_sc[mt * 128:(mt + 1) * 128,
                                     sg * SEG:(sg + 1) * SEG],
                        in_=INT5))
                    e.wait(["vector"], qs, qs.n)
                ev_mark5[mt] = ev.n
            join()

            IB5 = carve(96 * 1024, [2, 16, SEG], BF16)
            WD5 = carve(128 * 1024, [2, 16, 128], BF16)
            X3S = carve(0, [NCH, SEG], F32)

            for sego in range(2):
                e.wait(["sync"], qs, qs.n)
                e.wait(["sync"], ev, ev.n)
                rr(lambda ro: qs.dma(e.Y.dma_start(out=X3S, in_=x2_src(ro, sego))),
                   "sync")
                for hb in range(16):
                    bank = hb % 2
                    if hb >= 2:
                        e.wait(["tensor"], ev, ev.n)
                    for ktb in range(8):
                        bb = ktb % 2
                        e.wait(["sync"], mm, mm.n)
                        qs.dma(e.Y.dma_start(
                            out=IB5[:, bb, :, :],
                            in_=inter_sc.rearrange("(kb g p) t -> p kb g t", p=128, g=16)
                            [:, ktb, :, sego * SEG:(sego + 1) * SEG]))
                        qs.dma(e.Y.dma_start(
                            out=WD5[:, bb, :, :],
                            in_=sh_wd.rearrange("(kb g p) h -> p kb g h", p=128, g=16)
                            [:, ktb, :, hb * 128:(hb + 1) * 128]))
                        e.wait(["tensor"], qs, qs.n)
                        for g in range(16):
                            ins = e.T.matmul(ps[:, bank, :], WD5[:, bb, g, :],
                                             IB5[:, bb, g, :],
                                             start=(ktb == 0 and g == 0),
                                             stop=(ktb == 7 and g == 15))
                        mm.inc(ins)
                    e.wait(["vector"], mm, mm.n)
                    e.wait(["vector"], qs, qs.n)
                    ev.inc(e.V.scalar_tensor_tensor(
                        out=X3S[:, hb, :], in0=ps[:, bank, :], scalar=0.5,
                        in1=X3S[:, hb, :], op0=ALU.mult, op1=ALU.add))
                ev.inc(e.V.engine_nop())
                e.wait(["sync"], ev, ev.n)
                qs.dma(e.Y.dma_start(
                    out=out_t.rearrange("(c p) t -> p c t", p=128)
                    [:, :, sego * SEG:(sego + 1) * SEG],
                    in_=X3S))
                e.wait(["vector"], qs, qs.n)
            e.wait(ALL, qs, qs.n)
            e.wait(["gpsimd"], qg, qg.n)

        _sem_final = {}

        def section(eng_obj, name):
            sems = [Sem(h) for h in (qs_h, qg_h, mm_h, ev_h, es_h, ph_h, fg_h)]
            if name == "gpsimd":
                sems[1].dma(eng_obj.dma_start(out=s_role[:], in_=role_in[:]))
            else:
                sems[1].n += 16
            eng_obj.wait_ge(qg_h, 16)
            emit_body(E(nc, name), sems)
            _sem_final[name] = tuple(s.n for s in sems)

        nc._sem_final = _sem_final

        @blk.tensor
        def _(t):
            section(t, "tensor")

        @blk.vector
        def _(v):
            section(v, "vector")

        @blk.scalar
        def _(s):
            section(s, "scalar")

        @blk.gpsimd
        def _(g):
            section(g, "gpsimd")

        @blk.sync
        def _(s):
            section(s, "sync")

    return nc



def kernel(**inputs):
    return _run(inputs, debug=False)


def _run(inputs, debug=False):
    P = np.concatenate([np.arange(0, DM, 2), np.arange(1, DM, 2)])
    Q = np.concatenate([np.arange(0, MLP, 2), np.arange(1, MLP, 2)])

    x = np.asarray(inputs["L0_attn_x"])
    w_in = np.asarray(inputs["L0_attn_W_in"])
    conv_w = np.asarray(inputs["L0_attn_conv_w"])
    conv_b = np.asarray(inputs["L0_attn_conv_b"])
    w_x = np.asarray(inputs["L0_attn_W_x"])
    w_dt = np.asarray(inputs["L0_attn_W_dt"])
    b_dt = np.asarray(inputs["L0_attn_b_dt"])
    a_log = np.asarray(inputs["L0_attn_A_log"])
    d_in = np.asarray(inputs["L0_attn_D"])
    w_out = np.asarray(inputs["L0_attn_W_out"])
    g_pk = np.asarray(inputs["L0_ffn_gate_W_packed"])
    g_sc = np.asarray(inputs["L0_ffn_gate_scales"])
    u_pk = np.asarray(inputs["L0_ffn_up_W_packed"])
    u_sc = np.asarray(inputs["L0_ffn_up_scales"])
    d_pk = np.asarray(inputs["L0_ffn_down_W_packed"])
    d_sc = np.asarray(inputs["L0_ffn_down_scales"])

    asc = np.ascontiguousarray

    def pack_u8_T(pk_i32, row_perm):
        b = pk_i32[row_perm].view(np.uint8)[:, ::4]
        return asc(b.T)

    gate_pkT = pack_u8_T(g_pk, Q)
    up_pkT = pack_u8_T(u_pk, Q)
    down_pkT = pack_u8_T(d_pk, P)
    gsT = asc(g_sc[Q].T.astype(np.float32))
    usT = asc(u_sc[Q].T.astype(np.float32))
    dsT = asc(d_sc[P].T.astype(np.float32))

    w_in_p = asc(w_in[P])
    conv_wT = asc(conv_w.transpose(2, 1, 0))
    w_out_p = asc(w_out[:, P])

    def r128(v):
        return asc(v.reshape(NCH, 128).T.astype(np.float32))

    conv_b_r = r128(conv_b)
    b_dt_r = r128(b_dt)
    d_r = r128(d_in)

    consts = np.zeros((128, 32), dtype=np.float32)
    for ci, v in enumerate(CONST_VALS):
        consts[:, ci] = v
    iota = np.arange(64, dtype=np.int32).reshape(1, 64)
    ones_col = np.full((128, 1), 1.0 / DM, dtype=ml_dtypes.bfloat16)
    ones_row = np.ones((1, 128), dtype=ml_dtypes.bfloat16)
    ones_f32 = np.ones((128, SEG), dtype=np.float32)

    in_maps = []
    for c in range(8):
        pair = c // 2
        parity = c % 2
        batch = pair // 2
        island = pair % 2
        role = 2 * island + parity
        xb = x[batch].T[P]
        if island == 0:
            xs_ = xb[:, parity * PROD:(parity + 1) * PROD] if parity == 0 \
                else np.zeros((DM, PROD), dtype=np.float32)
        else:
            xs_ = xb[:, parity * PROD:(parity + 1) * PROD]
        a_half = a_log[:, parity * 8:(parity + 1) * 8]
        a_log_r = asc(a_half.reshape(NCH, 128, 8).transpose(1, 0, 2)
                      .reshape(128, NCH * 8).astype(np.float32))
        in_maps.append({
            "x_prod": asc(xs_.astype(np.float32)),
            "w_in": w_in_p, "conv_wT": conv_wT, "conv_b_r": conv_b_r,
            "w_x": asc(w_x.astype(np.float32)), "w_dt": asc(w_dt.astype(np.float32)),
            "b_dt_r": b_dt_r, "a_log_r": a_log_r, "d_r": d_r, "w_out": w_out_p,
            "gate_pkT": gate_pkT, "up_pkT": up_pkT, "down_pkT": down_pkT,
            "gsT": gsT, "usT": usT, "dsT": dsT,
            "kconsts": consts, "iota": iota, "ones_col": ones_col,
            "ones_row": ones_row, "ones_f32": ones_f32,
            "role": np.array([[role, parity, island] + [0] * 13, ], dtype=np.int32),
        })

    nc = build_program(debug=debug)
    res = run_bass_kernel_spmd(nc, in_maps, core_ids=list(range(8)))

    if debug:
        globals()["_dbg_res"] = [np.asarray(r["dbg"]) for r in res.results]

    invP = np.argsort(P)
    out = np.empty((2, LSEQ, DM), dtype=np.float32)
    for c in range(8):
        pair = c // 2
        parity = c % 2
        batch = pair // 2
        island = pair % 2
        t0 = 2048 * island + 1024 * parity
        o = np.asarray(res.results[c]["out"])
        out[batch, t0:t0 + OWN, :] = o[invP].T
    return out


# version 05
# speedup vs baseline: 1.6627x; 1.6627x over previous
import numpy as np
import ml_dtypes
import concourse.bass as bass
import concourse.mybir as mybir
from concourse.bass_utils import run_bass_kernel_spmd

F32 = mybir.dt.float32
BF16 = mybir.dt.bfloat16
I32 = mybir.dt.int32
I16 = mybir.dt.int16
U8 = mybir.dt.uint8
AF = mybir.ActivationFunctionType
ALU = mybir.AluOpType

DM = 2048
DIN = 2048
MLP = 16384
LSEQ = 4096
DTR = 128
KCONV = 4
PROD = 2048
OWN = 1024
SPAN = 4096
SEG = 512
NSEG = PROD // SEG
NCH = DM // 128
NMT = MLP // 128
EPS = 1e-5
GELU_C = 0.7978845608028654

CONST_VALS = [129.53125, -130.0, -4.0, -6.0, 0.0625, 0.125, -1.0, 1.0, 0.5,
              -0.5, 2.0, -2.0, -16.0, -8.0, 8.0, 64.0, 16128.0, 0.0, EPS,
              1.0 / DM, GELU_C, 0.044715]

DBG = False
SKIP_P0 = False

FLG_DEC = 0
FLG_XZ = 1
FLG_PRESCAN = 2
FLG_Y = 3
FLG_X1 = 4
NFLAG = 5


class NullIns:
    def then_inc(self, *a, **k):
        return self

    def __enter__(self):
        return self

    def __exit__(self, *a):
        return False


class NullEng:
    def __getattr__(self, name):
        return lambda *a, **k: NullIns()


class Sem:
    def __init__(self, handle):
        self.h = handle
        self.n = 0

    def inc(self, ins, k=1):
        ins.then_inc(self.h, k)
        self.n += k
        return self.n

    def dma(self, ins):
        return self.inc(ins, 16)


class E:

    ENGS = ("tensor", "vector", "scalar", "gpsimd", "sync")

    def __init__(self, nc, active):
        self.nc = nc
        self.active = active
        self._null = NullEng()

    def on(self, name):
        return getattr(self.nc, name) if name == self.active else self._null

    @property
    def T(self):
        return self.on("tensor")

    @property
    def V(self):
        return self.on("vector")

    @property
    def S(self):
        return self.on("scalar")

    @property
    def G(self):
        return self.on("gpsimd")

    @property
    def Y(self):
        return self.on("sync")

    def wait(self, names, sem, val):
        for nm in names:
            self.on(nm).wait_ge(sem.h, val)


def build_program(debug=DBG):
    nc = bass.Bass("TRN2", target_bir_lowering=False)

    def ein(name, shape, dt):
        return nc.dram_tensor(name, shape, dt, kind="ExternalInput").ap()

    x_in = ein("x_prod", [DM, PROD], F32)
    w_in_t = ein("w_in", [DM, 2 * DIN], F32)
    conv_wT = ein("conv_wT", [KCONV, DIN, DIN], F32)
    conv_b_r = ein("conv_b_r", [128, NCH], F32)
    w_x_t = ein("w_x", [DIN, DTR + 2 * 16], F32)
    w_dt_t = ein("w_dt", [DTR, DIN], F32)
    b_dt_r = ein("b_dt_r", [128, NCH], F32)
    a_log_r = ein("a_log_r", [128, NCH * 8], F32)
    d_r = ein("d_r", [128, NCH], F32)
    w_out_t = ein("w_out", [DIN, DM], F32)
    gate_pkT = ein("gate_pkT", [DM // 2, MLP], U8)
    up_pkT = ein("up_pkT", [DM // 2, MLP], U8)
    down_pkT = ein("down_pkT", [MLP // 2, DM], U8)
    gsT = ein("gsT", [DM // 32, MLP], F32)
    usT = ein("usT", [DM // 32, MLP], F32)
    dsT = ein("dsT", [MLP // 32, DM], F32)
    consts_in = ein("kconsts", [128, 32], F32)
    iota_in = ein("iota", [1, 64], I32)
    ones_col_in = ein("ones_col", [128, 1], BF16)
    ones_row_in = ein("ones_row", [1, 128], BF16)
    ones_f32_in = ein("ones_f32", [128, SEG], F32)
    role_in = ein("role", [1, 16], I32)

    out_t = nc.dram_tensor("out", [DM, OWN], F32, kind="ExternalOutput").ap()
    dbg_t = nc.dram_tensor("dbg", [DM, 20 * SEG], F32,
                           kind="ExternalOutput").ap() if debug else None

    def sh(name, shape, dt):
        return nc.dram_tensor(name, shape, dt, kind="Internal",
                              addr_space="Shared").ap()

    sh_wg = sh("sh_wg", [DM, MLP], BF16)
    sh_wu = sh("sh_wu", [DM, MLP], BF16)
    sh_wd = sh("sh_wd", [MLP, DM], BF16)
    sh_x1 = sh("sh_x1", [DM, SPAN], F32)
    sh_xc = sh("sh_xc", [DIN, SPAN], BF16)
    sh_r = sh("sh_r", [DIN, SPAN], BF16)
    sh_xs = sh("sh_xs", [DIN, SPAN], F32)
    sh_u = sh("sh_u", [DIN, SPAN], BF16)
    sh_delta = sh("sh_delta", [DIN, SPAN], F32)
    sh_bc = sh("sh_bc", [32, SPAN], F32)
    sh_y0 = sh("sh_y0", [DIN, SPAN], F32)
    sh_y1 = sh("sh_y1", [DIN, SPAN], F32)
    sh_flags = nc.dram_tensor("sh_flags", [2 * NFLAG, 16], I32, kind="Internal",
                              addr_space="Shared").ap()
    inter_sc = nc.dram_tensor("inter_sc", [MLP, PROD], BF16, kind="Internal").ap()

    ARENA_B = 149 * 1024

    import contextlib
    with contextlib.ExitStack() as _stk:
        def _e(cm):
            return _stk.enter_context(cm)
        arena = _e(nc.sbuf_tensor("arena", [128, ARENA_B], U8))
        s_consts = _e(nc.sbuf_tensor("s_consts", [128, 32], F32))
        s_role = _e(nc.sbuf_tensor("s_role", [1, 16], I32))
        s_iota = _e(nc.sbuf_tensor("s_iota", [1, 64], I32))
        s_onec = _e(nc.sbuf_tensor("s_onec", [128, 1], BF16))
        s_oner = _e(nc.sbuf_tensor("s_oner", [1, 128], BF16))
        s_A = _e(nc.sbuf_tensor("s_A", [128, NCH * 8], F32))
        s_alog = _e(nc.sbuf_tensor("s_alog", [128, NCH * 8], F32))
        s_D = _e(nc.sbuf_tensor("s_D", [128, NCH], F32))
        s_cb = _e(nc.sbuf_tensor("s_cb", [128, NCH], F32))
        s_bdt = _e(nc.sbuf_tensor("s_bdt", [128, NCH], F32))
        s_state = _e(nc.sbuf_tensor("s_state", [128, NCH * 8], F32))
        s_rows = _e(nc.sbuf_tensor("s_rows", [1, 8 * SEG], F32))
        s_bcr = _e(nc.sbuf_tensor("s_bcr", [32, SEG], F32))
        s_wxw = _e(nc.sbuf_tensor("s_wxw", [128, NCH * 160], BF16))
        s_wdtw = _e(nc.sbuf_tensor("s_wdtw", [128, NCH * 128], BF16))
        ps = _e(nc.psum_tensor("ps", [128, 8, SEG], F32))
        qs_h = _e(nc.semaphore("qs"))
        qg_h = _e(nc.semaphore("qg"))
        mm_h = _e(nc.semaphore("mm"))
        ev_h = _e(nc.semaphore("ev"))
        es_h = _e(nc.semaphore("es"))
        ph_h = _e(nc.semaphore("ph"))
        fg_h = _e(nc.semaphore("fg"))
        qw_h = _e(nc.semaphore("qw"))
        qi_h = _e(nc.semaphore("qi"))
        ep_h = _e(nc.semaphore("ep"))
        blk = _e(nc.Block())

        for ci, v in enumerate(CONST_VALS):
            nc.const_aps.register(F32, float(v), s_consts[:, ci:ci + 1])

        def carve(off, shape, dt):
            esz = {F32: 4, BF16: 2, I32: 4, U8: 1, I16: 2}[dt]
            nb = int(np.prod(shape)) * esz
            assert off % 512 == 0 and off + nb <= ARENA_B, (off, shape, nb)
            v = arena.ap()[:, off:off + nb].bitcast(dt)
            if len(shape) == 1:
                return v
            names = " ".join(f"d{i}" for i in range(len(shape)))
            v = v.rearrange(f"p ({names}) -> p {names}",
                            **{f"d{i}": s for i, s in enumerate(shape)})
            return v

        def emit_body(e: E, sems):
            qs, qg, mm, ev, es, ph, fg, qw, qi, ep = sems
            ALL = list(E.ENGS)

            aeng = e.on(e.active)
            par_reg = aeng.to_reg(aeng.value_load(s_role[0:1, 1:2]))
            isl_reg = aeng.to_reg(aeng.value_load(s_role[0:1, 2:3]))
            poll_c = aeng.alloc_register()
            poll_f = aeng.alloc_register()

            all_sems = sems

            def _snap():
                return [s.n for s in all_sems]

            def _restore(v):
                for s, x in zip(all_sems, v):
                    s.n = x

            def pp(fn, eng_name):
                en = e.on(eng_name)
                if eng_name != e.active:
                    fn(0)
                    return
                s0 = _snap()
                with en.If_eq(par_reg, 0):
                    fn(0)
                s1 = _snap()
                _restore(s0)
                with en.Else():
                    fn(1)
                assert _snap() == s1, "pp variants count differently"

            def rr(fn, eng_name):
                en = e.on(eng_name)
                if eng_name != e.active:
                    fn(0)
                    return
                s0 = _snap()
                with en.If_eq(isl_reg, 0):
                    with en.If_eq(par_reg, 0):
                        fn(0)
                    with en.Else():
                        _restore(s0)
                        fn(1)
                s1 = _snap()
                with en.Else():
                    with en.If_eq(par_reg, 0):
                        _restore(s0)
                        fn(2)
                    with en.Else():
                        _restore(s0)
                        fn(3)
                assert _snap() == s1, "rr variants count differently"

            def flag_set(fid, value):
                def go(p):
                    qg.dma(e.G.dma_start(
                        out=sh_flags[2 * fid + p:2 * fid + p + 1, 0:1],
                        in_=s_iota[0:1, value:value + 1]))
                pp(go, "gpsimd")

            def flag_poll(fid, value):
                def go(p):
                    g = e.G
                    row = 2 * fid + (1 - p)
                    def cond():
                        g.reg_load(poll_f, sh_flags[row:row + 1, 0:1])
                        g.reg_alu(poll_c, poll_f, value, ALU.is_lt)
                        return poll_c
                    with g.While(cond):
                        g.engine_nop()
                pp(go, "gpsimd")
                return fg.inc(e.G.engine_nop())

            def join():
                for en in ALL:
                    ph.inc(e.on(en).nop())
                t = ph.n
                for en in ALL:
                    e.on(en).wait_ge(ph.h, t)

            def dump(slot, src_ap):
                if not debug:
                    return
                qg.dma(e.G.dma_start(
                    out=dbg_t.rearrange("(c p) t -> p c t", p=128)
                    [:, :, slot * SEG:(slot + 1) * SEG], in_=src_ap))

            qs.dma(e.Y.dma_start(out=s_consts[:], in_=consts_in[:]))
            qs.dma(e.Y.dma_start(out=s_iota[:], in_=iota_in[:]))
            qs.dma(e.Y.dma_start(out=s_onec[:], in_=ones_col_in[:]))
            qs.dma(e.Y.dma_start(out=s_oner[:], in_=ones_row_in[:]))
            qs.dma(e.Y.dma_start(out=s_D[:], in_=d_r[:]))
            qs.dma(e.Y.dma_start(out=s_cb[:], in_=conv_b_r[:]))
            qs.dma(e.Y.dma_start(out=s_bdt[:], in_=b_dt_r[:]))
            qs.dma(e.Y.dma_start(out=s_alog[:], in_=a_log_r[:]))
            e.wait(ALL, qs, qs.n)
            es.inc(e.S.activation(out=s_A[:], in_=s_alog[:], func=AF.Exp))
            e.wait(["vector"], es, es.n)
            ev.inc(e.V.tensor_scalar(out=s_A[:], in0=s_A[:], scalar1=-1.0,
                                     scalar2=None, op0=ALU.mult))
            e.wait(ALL, ev, ev.n)

            MC = 2048
            t_pk = carve(0, [MC], BF16)
            t_hi = carve(4 * 1024, [MC], BF16)
            t_lo = carve(8 * 1024, [MC], BF16)
            t_sg = carve(12 * 1024, [MC], BF16)
            t_rl = carve(16 * 1024, [MC], F32)
            t_i16 = carve(24 * 1024, [MC], I16)
            t_o1 = carve(28 * 1024, [MC], BF16)
            t_o2 = carve(32 * 1024, [MC], BF16)
            t_sc = carve(36 * 1024, [MC], F32)
            t_scb = carve(44 * 1024, [MC], BF16)

            def nib_decode(vin, vout):
                ev.inc(e.V.tensor_scalar(out=t_sg, in0=vin, scalar1=0.125,
                                         scalar2=129.53125, op0=ALU.mult, op1=ALU.add))
                ev.inc(e.V.tensor_scalar(out=t_sg, in0=t_sg, scalar1=1.0,
                                         scalar2=-130.0, op0=ALU.mult, op1=ALU.add))
                ev.inc(e.V.scalar_tensor_tensor(out=vout, in0=t_sg, scalar=-8.0,
                                                in1=vin, op0=ALU.mult, op1=ALU.add))
                ev.inc(e.V.tensor_scalar(out=t_i16, in0=vout, scalar1=64.0,
                                         scalar2=16128.0, op0=ALU.mult, op1=ALU.add))
                fb = t_i16.bitcast(BF16)
                e.wait(["scalar"], ev, ev.n)
                es.inc(e.S.activation(out=t_rl, in_=fb, func=AF.Relu,
                                      scale=-1.0, bias=1.0))
                e.wait(["vector"], es, es.n)
                ev.inc(e.V.tensor_sub(out=vout, in0=fb, in1=t_rl))
                ev.inc(e.V.tensor_scalar(out=t_sg, in0=t_sg, scalar1=-2.0,
                                         scalar2=1.0, op0=ALU.mult, op1=ALU.add))
                ev.inc(e.V.tensor_mul(out=vout, in0=vout, in1=t_sg))
                ev.inc(e.V.tensor_mul(out=vout, in0=vout, in1=t_scb))

            def decode_block(pk_ap, scT_ap, j0_src, jrows, mcols,
                             dst_fn_lo, dst_fn_hi):
                for jt in range(jrows // 128):
                    j0 = j0_src + jt * 128
                    for mt in range(mcols // MC):
                        m0 = mt * MC
                        e.wait(["gpsimd"], ev, ev.n)
                        e.wait(["sync"], ev, ev.n)
                        qg.dma(e.G.dma_start(out=t_pk,
                                             in_=pk_ap[j0:j0 + 128, m0:m0 + MC]))
                        sc_src = scT_ap[j0 // 16:j0 // 16 + 8, m0:m0 + MC]
                        sc_b = sc_src[:, None, :].broadcast_to((8, 16, MC))
                        qs.dma(e.Y.dma_start(out=t_sc, in_=sc_b))
                        e.wait(["vector"], qg, qg.n)
                        e.wait(["vector"], qs, qs.n)
                        ev.inc(e.V.tensor_copy(out=t_scb, in_=t_sc))
                        ev.inc(e.V.tensor_scalar(out=t_hi, in0=t_pk, scalar1=0.0625,
                                                 scalar2=129.53125, op0=ALU.mult,
                                                 op1=ALU.add))
                        ev.inc(e.V.tensor_scalar(out=t_hi, in0=t_hi, scalar1=1.0,
                                                 scalar2=-130.0, op0=ALU.mult,
                                                 op1=ALU.add))
                        ev.inc(e.V.scalar_tensor_tensor(out=t_lo, in0=t_hi,
                                                        scalar=-16.0, in1=t_pk,
                                                        op0=ALU.mult, op1=ALU.add))
                        nib_decode(t_lo, t_o1)
                        nib_decode(t_hi, t_o2)
                        e.wait(["sync"], ev, ev.n)
                        qs.dma(e.Y.dma_start(out=dst_fn_lo(jt, m0), in_=t_o1))
                        qs.dma(e.Y.dma_start(out=dst_fn_hi(jt, m0), in_=t_o2))
                        e.wait(["vector"], qs, qs.n)

            def p0(p):
                pk = gate_pkT if p == 0 else up_pkT
                sc = gsT if p == 0 else usT
                wdst = sh_wg if p == 0 else sh_wu
                decode_block(pk, sc, 0, DM // 2, MLP,
                             lambda jt, m0: wdst[jt * 128:jt * 128 + 128, m0:m0 + MC],
                             lambda jt, m0: wdst[DM // 2 + jt * 128:DM // 2 + jt * 128 + 128,
                                                 m0:m0 + MC])
                jb = p * (MLP // 4)
                decode_block(down_pkT, dsT, jb, MLP // 4, DM,
                             lambda jt, m0: sh_wd[jb + jt * 128:jb + jt * 128 + 128,
                                                  m0:m0 + MC],
                             lambda jt, m0: sh_wd[MLP // 2 + jb + jt * 128:
                                                  MLP // 2 + jb + jt * 128 + 128,
                                                  m0:m0 + MC])

            skip_p0 = SKIP_P0
            saved = tuple(s.n for s in all_sems)

            def run_p0(p):
                for s, v in zip(all_sems, saved):
                    s.n = v
                p0(p)

            if not skip_p0:
                if e.active in ("vector", "scalar", "gpsimd", "sync"):
                    en = e.on(e.active)
                    with en.If_eq(par_reg, 0):
                        run_p0(0)
                    with en.Else():
                        run_p0(1)
                else:
                    run_p0(0)

                e.wait(["gpsimd"], qs, qs.n)
                e.wait(["gpsimd"], qg, qg.n)
                flag_set(FLG_DEC, 1)
                n = flag_poll(FLG_DEC, 1)
                e.wait(ALL, fg, n)
            join()

            XSEGL = carve(64 * 1024, [NCH, SEG], F32)
            XBF = carve(96 * 1024, [NCH, SEG], BF16)
            SQBF = carve(112 * 1024, [NCH, SEG], BF16)

            def ln_seg2(src_emit, xn_ap, dst_off):
                e.wait(["sync"], ev, ev.n)
                e.wait(["scalar"], ev, ev.n)
                e.wait(["scalar"], mm, mm.n)
                src_emit()
                e.wait(["scalar", "vector"], qs, qs.n)
                for c in range(NCH):
                    es.inc(e.S.activation(out=XBF[:, c, :], in_=XSEGL[:, c, :],
                                          func=AF.Identity))
                    es.inc(e.S.activation(out=SQBF[:, c, :], in_=XSEGL[:, c, :],
                                          func=AF.Square))
                e.wait(["tensor"], es, es.n)
                for c in range(NCH):
                    e.T.matmul(ps[0:1, 6, :], s_onec[:, 0:1], XBF[:, c, :],
                               start=(c == 0), stop=(c == NCH - 1))
                for c in range(NCH):
                    ins = e.T.matmul(ps[0:1, 7, :], s_onec[:, 0:1], SQBF[:, c, :],
                                     start=(c == 0), stop=(c == NCH - 1))
                mm.inc(ins)
                e.wait(["vector"], mm, mm.n)
                ev.inc(e.V.tensor_copy(out=s_rows[0:1, 0:SEG], in_=ps[0:1, 6, :]))
                ev.inc(e.V.scalar_tensor_tensor(out=s_rows[0:1, SEG:2 * SEG],
                                                in0=s_rows[0:1, 0:SEG], scalar=-1.0,
                                                in1=s_rows[0:1, 0:SEG],
                                                op0=ALU.mult, op1=ALU.mult))
                ev.inc(e.V.tensor_add(out=s_rows[0:1, SEG:2 * SEG],
                                      in0=s_rows[0:1, SEG:2 * SEG], in1=ps[0:1, 7, :]))
                e.wait(["scalar"], ev, ev.n)
                es.inc(e.S.activation(out=s_rows[0:1, 2 * SEG:3 * SEG],
                                      in_=s_rows[0:1, SEG:2 * SEG], func=AF.Sqrt,
                                      bias=EPS))
                e.wait(["vector"], es, es.n)
                ev.inc(e.V.reciprocal(out=s_rows[0:1, 2 * SEG:3 * SEG],
                                      in_=s_rows[0:1, 2 * SEG:3 * SEG]))
                ev.inc(e.V.scalar_tensor_tensor(out=s_rows[0:1, 3 * SEG:4 * SEG],
                                                in0=s_rows[0:1, 0:SEG], scalar=-1.0,
                                                in1=s_rows[0:1, 2 * SEG:3 * SEG],
                                                op0=ALU.mult, op1=ALU.mult))
                e.wait(["tensor"], ev, ev.n)
                e.wait(["scalar"], ev, ev.n)
                es.inc(e.S.activation(out=s_rows[0:1, 4 * SEG:5 * SEG].bitcast(BF16)[0:1, 0:SEG],
                                      in_=s_rows[0:1, 2 * SEG:3 * SEG], func=AF.Identity))
                es.inc(e.S.activation(out=s_rows[0:1, 4 * SEG:5 * SEG].bitcast(BF16)[0:1, SEG:2 * SEG],
                                      in_=s_rows[0:1, 3 * SEG:4 * SEG], func=AF.Identity))
                e.wait(["tensor"], es, es.n)
                rows_bf = s_rows[0:1, 4 * SEG:5 * SEG].bitcast(BF16)
                e.T.matmul(ps[:, 6, :], s_oner[0:1, :], rows_bf[0:1, 0:SEG],
                           start=True, stop=True)
                ins = e.T.matmul(ps[:, 7, :], s_oner[0:1, :], rows_bf[0:1, SEG:2 * SEG],
                                 start=True, stop=True)
                mm.inc(ins)
                e.wait(["vector"], mm, mm.n)
                for c in range(NCH):
                    ev.inc(e.V.tensor_mul(out=XBF[:, c, :], in0=XSEGL[:, c, :],
                                          in1=ps[:, 6, :]))
                    ev.inc(e.V.tensor_add(out=xn_ap[:, c, dst_off:dst_off + SEG],
                                          in0=XBF[:, c, :], in1=ps[:, 7, :]))
                ev.inc(e.V.engine_nop())
                e.wait(["tensor"], ev, ev.n)

            def ffn_gate_up(xn_ap, tspan):
                WGB = carve(64 * 1024, [2, NCH, SEG], BF16)
                WUB = carve(96 * 1024, [2, NCH, SEG], BF16)
                GEL = carve(128 * 1024, [2, SEG], F32)
                UPS = carve(132 * 1024, [2, SEG], BF16)
                INTB = carve(134 * 1024, [2, SEG], BF16)
                NSGt = tspan // SEG
                gmark = {}
                gel_mark = [0, 0]
                int_mark = [0, 0]
                store_mark = [0, 0]
                for mg in range(MLP // SEG):
                    b = mg % 2
                    if mg >= 2:
                        e.wait(["sync"], mm, gmark[mg - 2])
                    qw.dma(e.Y.dma_start(
                        out=WGB[:, b, :, :],
                        in_=sh_wg.rearrange("(c p) m -> p c m", p=128)
                        [:, :, mg * SEG:(mg + 1) * SEG]))
                    qw.dma(e.Y.dma_start(
                        out=WUB[:, b, :, :],
                        in_=sh_wu.rearrange("(c p) m -> p c m", p=128)
                        [:, :, mg * SEG:(mg + 1) * SEG]))
                    e.wait(["tensor"], qw, qw.n)
                    for mt4 in range(4):
                        mt = mg * 4 + mt4
                        for sg in range(NSGt):
                            a = sg % 2
                            e.wait(["tensor"], es, gel_mark[a])
                            for c in range(NCH):
                                e.T.matmul(ps[:, a, :],
                                           WGB[:, b, c, mt4 * 128:(mt4 + 1) * 128],
                                           xn_ap[:, c, sg * SEG:(sg + 1) * SEG],
                                           start=(c == 0), stop=(c == NCH - 1))
                            for c in range(NCH):
                                ins = e.T.matmul(ps[:, 2 + a, :],
                                                 WUB[:, b, c, mt4 * 128:(mt4 + 1) * 128],
                                                 xn_ap[:, c, sg * SEG:(sg + 1) * SEG],
                                                 start=(c == 0), stop=(c == NCH - 1))
                            mm.inc(ins)
                            e.wait(["scalar"], mm, mm.n)
                            e.wait(["scalar"], ep, int_mark[a])
                            es.inc(e.S.activation(out=GEL[:, a, :], in_=ps[:, a, :],
                                                  func=AF.Gelu_apprx_tanh))
                            es.inc(e.S.activation(out=UPS[:, a, :], in_=ps[:, 2 + a, :],
                                                  func=AF.Identity))
                            gel_mark[a] = es.n
                            e.wait(["gpsimd"], es, es.n)
                            e.wait(["gpsimd"], qs, store_mark[a])
                            ep.inc(e.G.tensor_mul(out=INTB[:, a, :], in0=GEL[:, a, :],
                                                  in1=UPS[:, a, :]))
                            int_mark[a] = ep.n
                            e.wait(["sync"], ep, ep.n)
                            qs.dma(e.Y.dma_start(
                                out=inter_sc[mt * 128:(mt + 1) * 128,
                                             sg * SEG:(sg + 1) * SEG],
                                in_=INTB[:, a, :]))
                            store_mark[a] = qs.n
                    gmark[mg] = mm.n
                e.wait(["sync"], qs, qs.n)

            def ffn_down(tspan, xsrc_emit, store_emit):
                IQB = carve(0, [2, 16, SEG], BF16)
                WDS = carve(32 * 1024, [2, 16, 1024], BF16)
                PART = carve(96 * 1024, [NCH, SEG], F32)
                nsg = tspan // SEG
                iq_mark = {}
                slab_mark = {}
                slab_load = {}
                iq_load = {}
                gmarks = {}
                part_store = 0
                for seg in range(nsg):
                    e.wait(["sync"], qs, qs.n)
                    xsrc_emit(seg)
                    part_load = qs.n
                    for R in range(8):
                        gi = seg * 8 + R
                        bi = gi % 2
                        e.wait(["sync"], mm, iq_mark.get(gi - 2, 0))
                        qs.dma(e.Y.dma_start(
                            out=IQB[:, bi, :, :],
                            in_=inter_sc.rearrange("(J p) t -> p J t", p=128)
                            [:, R * 16:(R + 1) * 16, seg * SEG:(seg + 1) * SEG]))
                        iq_load[gi] = qs.n
                        for hbg in range(2):
                            si = 2 * gi + hbg
                            ws = si % 2
                            e.wait(["scalar"], mm, slab_mark.get(si - 2, 0))
                            qw.dma(e.S.dma_start(
                                out=WDS[:, ws, :, :],
                                in_=sh_wd.rearrange("(J p) h -> p J h", p=128)
                                [:, R * 16:(R + 1) * 16,
                                 hbg * 1024:(hbg + 1) * 1024]))
                            slab_load[si] = qw.n
                        for hbg in range(2):
                            si = 2 * gi + hbg
                            ws = si % 2
                            e.wait(["tensor"], qs, iq_load[gi])
                            e.wait(["tensor"], qw, slab_load[si])
                            e.wait(["tensor"], ev, ev.n)
                            for hbl in range(8):
                                for J in range(16):
                                    ins = e.T.matmul(ps[:, hbl, :],
                                                     WDS[:, ws, J,
                                                         hbl * 128:(hbl + 1) * 128],
                                                     IQB[:, bi, J, :],
                                                     start=(J == 0), stop=(J == 15))
                                mm.inc(ins)
                            slab_mark[si] = mm.n
                            e.wait(["vector"], mm, mm.n)
                            if gi == seg * 8 and hbg == 0:
                                e.wait(["vector"], qs, part_load)
                            for hbl in range(8):
                                ev.inc(e.V.scalar_tensor_tensor(
                                    out=PART[:, hbg * 8 + hbl, :],
                                    in0=ps[:, hbl, :], scalar=0.5,
                                    in1=PART[:, hbg * 8 + hbl, :],
                                    op0=ALU.mult, op1=ALU.add))
                            ev.inc(e.V.engine_nop())
                        iq_mark[gi] = mm.n
                    e.wait(["sync"], ev, ev.n)
                    store_emit(seg)
                e.wait(["sync"], qs, qs.n)

            XN = carve(0, [NCH, PROD], BF16)

            def x_src(p, seg):
                return x_in.rearrange("(c p) t -> p c t", p=128)[:, :, seg * SEG:(seg + 1) * SEG]

            for seg in range(NSEG):
                def ld(seg=seg):
                    pp(lambda p: qs.dma(e.Y.dma_start(out=XSEGL, in_=x_src(p, seg))),
                       "sync")
                ln_seg2(ld, XN, seg * SEG)
            join()
            dump(0, XN[:, :, 0:SEG])

            ffn_gate_up(XN, PROD)
            join()
            dump(1, inter_sc.rearrange("(c p) m -> p c m", p=128)[:, 0:16, 0:SEG])

            PARTD = carve(96 * 1024, [NCH, SEG], F32)

            def x1_prod_ap(p, seg):
                c0 = p * PROD + seg * SEG
                return sh_x1.rearrange("(c p) t -> p c t", p=128)[:, :, c0:c0 + SEG]

            def ffn1_xsrc(seg):
                pp(lambda p: qs.dma(e.Y.dma_start(out=PARTD, in_=x_src(p, seg))),
                   "sync")

            def ffn1_store(seg):
                pp(lambda p: qs.dma(e.Y.dma_start(out=x1_prod_ap(p, seg), in_=PARTD)),
                   "sync")

            ffn_down(PROD, ffn1_xsrc, ffn1_store)
            e.wait(["gpsimd"], qs, qs.n)
            flag_set(FLG_X1, 1)
            join()
            dump(2, sh_x1.rearrange("(c p) t -> p c t", p=128)[:, :, 0:SEG])

            for seg in range(NSEG):
                def ld(seg=seg):
                    pp(lambda p: qs.dma(e.Y.dma_start(out=XSEGL, in_=x1_prod_ap(p, seg))),
                       "sync")
                ln_seg2(ld, XN, seg * SEG)
            join()
            dump(3, XN[:, :, 0:SEG])

            WIB = carve(64 * 1024, [2, NCH, SEG], BF16)
            EVA = carve(96 * 1024, [2, SEG], BF16)
            wi_mark = {}
            eva_store = [0, 0]
            bank_mark = [0, 0]
            for og in range(8):
                bo = og % 2
                e.wait(["gpsimd"], mm, wi_mark.get(og - 2, 0))
                qi.dma(e.G.dma_start(
                    out=WIB[:, bo, :, :],
                    in_=w_in_t.rearrange("(c p) m -> p c m", p=128)
                    [:, :, og * SEG:(og + 1) * SEG]))
                e.wait(["tensor"], qi, qi.n)
                for ob4 in range(4):
                    ob = og * 4 + ob4
                    for seg in range(NSEG):
                        a = (NSEG * ob4 + seg) % 2
                        e.wait(["tensor"], es, bank_mark[a])
                        for c in range(NCH):
                            ins = e.T.matmul(ps[:, a, :],
                                             WIB[:, bo, c, ob4 * 128:(ob4 + 1) * 128],
                                             XN[:, c, seg * SEG:(seg + 1) * SEG],
                                             start=(c == 0), stop=(c == NCH - 1))
                        mm.inc(ins)
                        e.wait(["scalar"], mm, mm.n)
                        e.wait(["scalar"], qs, eva_store[a])
                        es.inc(e.S.activation(out=EVA[:, a, :], in_=ps[:, a, :],
                                              func=AF.Identity))
                        bank_mark[a] = es.n
                        e.wait(["sync"], es, es.n)
                        if ob < NCH:
                            dst = sh_xc
                            rw = ob
                        else:
                            dst = sh_r
                            rw = ob - NCH

                        def store(p, dst=dst, rw=rw, seg=seg, a=a):
                            c0 = p * PROD + seg * SEG
                            qs.dma(e.Y.dma_start(
                                out=dst[rw * 128:(rw + 1) * 128, c0:c0 + SEG],
                                in_=EVA[:, a, :]))
                        pp(store, "sync")
                        eva_store[a] = qs.n
                wi_mark[og] = mm.n
            e.wait(["gpsimd"], qs, qs.n)
            flag_set(FLG_XZ, 1)
            n = flag_poll(FLG_XZ, 1)
            e.wait(ALL, fg, n)
            join()
            dump(4, sh_xc.rearrange("(c p) t -> p c t", p=128)[:, :, 0:SEG])
            dump(12, sh_r.rearrange("(c p) t -> p c t", p=128)[:, :, 0:SEG])

            XCW = carve(0, [NCH, SEG + 4], BF16)
            CWB = carve(17 * 1024, [2, NCH, KCONV, 256], BF16)
            UEV = carve(84 * 1024, [SEG], BF16)
            XSEV = carve(86 * 1024, [SEG], F32)
            cv_mark = {}
            cbank = [0, 0]
            u_store = 0
            for obg in range(8):
                bo = obg % 2
                e.wait(["gpsimd"], mm, cv_mark.get(obg - 2, 0))
                for kk in range(KCONV):
                    qi.dma(e.G.dma_start(
                        out=CWB[:, bo, :, kk, :],
                        in_=conv_wT.rearrange("k (c p) o -> p k c o", p=128)
                        [:, kk, :, obg * 256:(obg + 1) * 256]))
                e.wait(["tensor"], qi, qi.n)
                if debug and obg < 2:
                    e.wait(["gpsimd"], qi, qi.n)
                    dump(14 + 2 * obg, CWB[:, bo, :, 0:2, :])
                    dump(15 + 2 * obg, CWB[:, bo, :, 2:4, :])
                    e.wait(["gpsimd"], qg, qg.n)
                for seg in range(NSEG):
                    e.wait(["vector"], mm, mm.n)
                    e.wait(["sync"], mm, mm.n)
                    ev.inc(e.V.memset(XCW[:, :, 0:4], 0.0))
                    e.wait(["sync"], ev, ev.n)

                    def ldwin(p, seg=seg):
                        c0 = p * PROD + seg * SEG
                        if c0 == 0:
                            qs.dma(e.Y.dma_start(
                                out=XCW[:, :, 4:SEG + 4],
                                in_=sh_xc.rearrange("(c p) t -> p c t", p=128)
                                [:, :, 0:SEG]))
                        else:
                            qs.dma(e.Y.dma_start(
                                out=XCW[:, :, 0:SEG + 4],
                                in_=sh_xc.rearrange("(c p) t -> p c t", p=128)
                                [:, :, c0 - 4:c0 + SEG]))
                    pp(ldwin, "sync")
                    e.wait(["tensor"], qs, qs.n)
                    if debug and obg == 0 and seg == 0:
                        e.wait(["gpsimd"], qs, qs.n)
                        dump(18, XCW[:, :, 4:4 + SEG])
                        e.wait(["gpsimd"], qg, qg.n)
                    for ob2 in range(2):
                        ob = obg * 2 + ob2
                        a = (2 * seg + ob2) % 2
                        e.wait(["tensor"], es, cbank[a])
                        for c in range(NCH):
                            for k in range(KCONV):
                                ins = e.T.matmul(ps[:, a, :],
                                                 CWB[:, bo, c, k,
                                                     ob2 * 128:(ob2 + 1) * 128],
                                                 XCW[:, c, 1 + k:1 + k + SEG],
                                                 start=(c == 0 and k == 0),
                                                 stop=(c == NCH - 1 and k == KCONV - 1))
                        mm.inc(ins)
                        e.wait(["scalar"], mm, mm.n)
                        e.wait(["scalar"], qs, u_store)
                        es.inc(e.S.activation(out=UEV, in_=ps[:, a, :],
                                              func=AF.Silu, bias=s_cb[:, ob:ob + 1]))
                        es.inc(e.S.activation(out=XSEV, in_=ps[:, a, :],
                                              func=AF.Identity, bias=s_cb[:, ob:ob + 1]))
                        cbank[a] = es.n
                        e.wait(["sync"], es, es.n)

                        def store_u(p, ob=ob, seg=seg):
                            c0 = p * PROD + seg * SEG
                            qs.dma(e.Y.dma_start(out=sh_u[ob * 128:(ob + 1) * 128,
                                                          c0:c0 + SEG],
                                                 in_=UEV))
                            qs.dma(e.Y.dma_start(out=sh_xs[ob * 128:(ob + 1) * 128,
                                                           c0:c0 + SEG],
                                                 in_=XSEV))
                        pp(store_u, "sync")
                        u_store = qs.n
                cv_mark[obg] = mm.n
            join()
            dump(5, sh_u.rearrange("(c p) t -> p c t", p=128)[:, :, 0:SEG])
            dump(13, sh_xs.rearrange("(c p) t -> p c t", p=128)[:, :, 0:SEG])

            qg.dma(e.G.dma_start(out=s_wxw.rearrange("p (c f) -> p c f", c=NCH),
                                 in_=w_x_t.rearrange("(c p) f -> p c f", p=128)))
            qg.dma(e.G.dma_start(out=s_wdtw.rearrange("p (c f) -> p c f", c=NCH),
                                 in_=w_dt_t.rearrange("p (c f) -> p c f", c=NCH)))
            e.wait(["tensor"], qg, qg.n)
            UCX = carve(0, [NCH, SEG], BF16)
            DBC = carve(16 * 1024, [SEG], BF16)
            DEL = carve(20 * 1024, [SEG], F32)
            DE2 = carve(24 * 1024, [SEG], F32)
            DE3 = carve(28 * 1024, [SEG], F32)

            for seg in range(NSEG):
                if seg > 0:
                    e.wait(["sync"], mm, mm.n)
                    e.wait(["sync"], es, es.n)

                def ld_u(p, seg=seg):
                    c0 = p * PROD + seg * SEG
                    qs.dma(e.Y.dma_start(out=UCX,
                                         in_=sh_u.rearrange("(c p) t -> p c t", p=128)
                                         [:, :, c0:c0 + SEG]))
                pp(ld_u, "sync")
                e.wait(["tensor"], qs, qs.n)
                if seg > 0:
                    e.wait(["tensor"], ev, ev.n)
                    e.wait(["tensor"], es, es.n)
                    e.wait(["scalar"], mm, mm.n)
                for c in range(NCH):
                    e.T.matmul(ps[:, 6, :], s_wxw.rearrange("p (c f) -> p c f", c=NCH)
                               [:, c, 0:DTR],
                               UCX[:, c, :],
                               start=(c == 0), stop=(c == NCH - 1))
                for c in range(NCH):
                    ins = e.T.matmul(ps[0:32, 7, :], s_wxw.rearrange("p (c f) -> p c f", c=NCH)
                                     [:, c, DTR:DTR + 32],
                                     UCX[:, c, :],
                                     start=(c == 0), stop=(c == NCH - 1))
                mm.inc(ins)
                e.wait(["scalar", "vector"], mm, mm.n)
                es.inc(e.S.activation(out=DBC, in_=ps[:, 6, :], func=AF.Identity))
                ev.inc(e.V.tensor_copy(out=s_bcr[:, :], in_=ps[0:32, 7, :]))
                e.wait(["sync"], ev, ev.n)

                def store_bc(p, seg=seg):
                    c0 = p * PROD + seg * SEG
                    qs.dma(e.Y.dma_start(out=sh_bc[:, c0:c0 + SEG], in_=s_bcr[:, :]))
                pp(store_bc, "sync")
                e.wait(["tensor"], es, es.n)
                e.wait(["tensor"], ev, ev.n)
                for ob in range(NCH):
                    if ob > 0:
                        e.wait(["tensor"], es, es.n)
                    ins = e.T.matmul(ps[:, 7, :],
                                     s_wdtw.rearrange("p (c f) -> p c f", c=NCH)[:, ob, :],
                                     DBC, start=True, stop=True)
                    mm.inc(ins)
                    e.wait(["scalar"], mm, mm.n)
                    es.inc(e.S.activation(out=DEL, in_=ps[:, 7, :], func=AF.Identity,
                                          bias=s_bdt[:, ob:ob + 1]))
                    e.wait(["vector"], es, es.n)
                    ev.inc(e.V.tensor_relu(out=DE2, in_=DEL))
                    e.wait(["scalar"], ev, ev.n)
                    es.inc(e.S.activation(out=DE3, in_=DEL, func=AF.Abs))
                    e.wait(["scalar"], es, es.n)
                    es.inc(e.S.activation(out=DE3, in_=DE3, func=AF.Exp, scale=-1.0))
                    es.inc(e.S.activation(out=DE3, in_=DE3, func=AF.Ln, bias=1.0))
                    e.wait(["vector"], es, es.n)
                    ev.inc(e.V.tensor_add(out=DEL, in0=DE2, in1=DE3))
                    e.wait(["sync"], ev, ev.n)

                    def store_del(p, ob=ob, seg=seg):
                        c0 = p * PROD + seg * SEG
                        qs.dma(e.Y.dma_start(
                            out=sh_delta[ob * 128:(ob + 1) * 128, c0:c0 + SEG],
                            in_=DEL))
                    pp(store_del, "sync")
                    e.wait(["scalar", "tensor"], qs, qs.n)
            e.wait(["gpsimd"], qs, qs.n)
            flag_set(FLG_PRESCAN, 1)
            n = flag_poll(FLG_PRESCAN, 1)
            e.wait(ALL, fg, n)
            join()
            dump(6, sh_delta.rearrange("(c p) t -> p c t", p=128)[:, :, 0:SEG])

            DELC = carve(0, [NCH, SEG], F32)
            UC = carve(32 * 1024, [NCH, SEG], BF16)
            DUC = carve(48 * 1024, [NCH, SEG], F32)
            YC = carve(80 * 1024, [NCH, SEG], F32)
            BCB = carve(112 * 1024, [2, 8, SEG], BF16)
            DAT = carve(128 * 1024, [2, SEG], F32)
            DBT1 = carve(132 * 1024, [2, SEG], F32)
            DBT2 = carve(136 * 1024, [2, SEG], F32)
            HT = carve(140 * 1024, [2, SEG], F32)

            ev.inc(e.V.memset(s_state[:, :], 0.0))
            e.wait(ALL, ev, ev.n)

            scan_mark = {}
            pool_mark = {}
            for tc in range(SPAN // SEG):
                c0 = tc * SEG
                e.wait(["sync"], ev, ev.n)
                e.wait(["sync"], es, es.n)
                e.wait(["sync"], ep, ep.n)
                qs.dma(e.Y.dma_start(out=DELC,
                                     in_=sh_delta.rearrange("(c p) t -> p c t", p=128)
                                     [:, :, c0:c0 + SEG]))
                qs.dma(e.Y.dma_start(out=UC,
                                     in_=sh_u.rearrange("(c p) t -> p c t", p=128)
                                     [:, :, c0:c0 + SEG]))
                e.wait(["gpsimd"], ev, ev.n)

                def ld_bc(p, c0=c0):
                    r0 = p * 8
                    qi.dma(e.G.dma_start(
                        out=BCB[:, 0, :, :],
                        in_=sh_bc[None, r0:r0 + 8, c0:c0 + SEG]
                        .broadcast_to((128, 8, SEG))))
                    qi.dma(e.G.dma_start(
                        out=BCB[:, 1, :, :],
                        in_=sh_bc[None, 16 + r0:16 + r0 + 8, c0:c0 + SEG]
                        .broadcast_to((128, 8, SEG))))
                pp(ld_bc, "gpsimd")
                e.wait(["vector", "scalar"], qs, qs.n)
                e.wait(["vector"], qi, qi.n)
                e.wait(["vector"], ep, ep.n)
                for c in range(NCH):
                    ev.inc(e.V.tensor_mul(out=DUC[:, c, :], in0=DELC[:, c, :],
                                          in1=UC[:, c, :]))

                def yinit(p):
                    for c in range(NCH):
                        if p == 0:
                            ev.inc(e.V.tensor_scalar(out=YC[:, c, :],
                                                     in0=UC[:, c, :],
                                                     scalar1=s_D[:, c:c + 1],
                                                     scalar2=None, op0=ALU.mult))
                        else:
                            ev.inc(e.V.memset(YC[:, c, :], 0.0))
                pp(yinit, "vector")
                ev.inc(e.V.engine_nop())
                yinit_mark = ev.n
                e.wait(["gpsimd"], ev, yinit_mark)
                e.wait(["scalar"], ev, yinit_mark)
                it = 0
                for nn in range(8):
                    for c in range(NCH):
                        col = c * 8 + nn
                        x = it % 2
                        e.wait(["scalar"], ev, scan_mark.get(it - 2, 0))
                        es.inc(e.S.activation(out=DAT[:, x, :], in_=DELC[:, c, :],
                                              func=AF.Exp,
                                              scale=s_A[:, col:col + 1]))
                        ev.inc(e.V.tensor_mul(out=DBT1[:, x, :], in0=DUC[:, c, :],
                                              in1=BCB[:, 0, nn, :]))
                        e.wait(["vector"], es, es.n)
                        e.wait(["vector"], ep, pool_mark.get(it - 2, 0))
                        ev.inc(e.V.tensor_tensor_scan(out=HT[:, x, :],
                                                      data0=DAT[:, x, :],
                                                      data1=DBT1[:, x, :],
                                                      initial=s_state[:, col:col + 1],
                                                      op0=ALU.mult, op1=ALU.add))
                        ev.inc(e.V.tensor_mul(out=DBT2[:, x, :], in0=HT[:, x, :],
                                              in1=BCB[:, 1, nn, :]))
                        scan_mark[it] = ev.n
                        e.wait(["gpsimd"], ev, scan_mark[it])
                        ep.inc(e.G.tensor_add(out=YC[:, c, :], in0=YC[:, c, :],
                                              in1=DBT2[:, x, :]))
                        ep.inc(e.G.tensor_copy(out=s_state[:, col:col + 1],
                                               in_=HT[:, x, SEG - 1:SEG]))
                        pool_mark[it] = ep.n
                        it += 1
                e.wait(["sync"], ep, ep.n)
                e.wait(["sync"], ev, ev.n)

                def store_y(p, tc=tc):
                    dst = sh_y0 if p == 0 else sh_y1
                    qs.dma(e.Y.dma_start(
                        out=dst.rearrange("(c p) t -> p c t", p=128)
                        [:, :, tc * SEG:(tc + 1) * SEG], in_=YC))
                pp(store_y, "sync")
                e.wait(["gpsimd"], qs, qs.n)
                flag_set(FLG_Y, tc + 1)
                e.wait(["vector"], qs, qs.n)
            n = flag_poll(FLG_Y, SPAN // SEG)
            e.wait(ALL, fg, n)
            nx1 = flag_poll(FLG_X1, 1)
            e.wait(ALL, fg, nx1)
            join()
            dump(7, sh_y0.rearrange("(c p) t -> p c t", p=128)[:, :, 0:SEG])
            dump(8, sh_y1.rearrange("(c p) t -> p c t", p=128)[:, :, 0:SEG])

            YM = carve(0, [NCH, SEG], F32)
            YP = carve(32 * 1024, [NCH, SEG], F32)
            GAT = carve(64 * 1024, [2, NCH, SEG], BF16)

            def own_off(role):
                return 1024 * role

            for sego in range(2):
                e.wait(["sync"], ev, ev.n)
                e.wait(["sync"], es, es.n)

                def ld_own(role, src, dst, sego=sego):
                    c0 = own_off(role) + sego * SEG
                    qs.dma(e.Y.dma_start(out=dst,
                                         in_=src.rearrange("(c p) t -> p c t", p=128)
                                         [:, :, c0:c0 + SEG]))
                rr(lambda ro: ld_own(ro, sh_y0, YM), "sync")
                rr(lambda ro: ld_own(ro, sh_y1, YP), "sync")
                e.wait(["vector"], qs, qs.n)
                for c in range(NCH):
                    ev.inc(e.V.tensor_add(out=YM[:, c, :], in0=YM[:, c, :],
                                          in1=YP[:, c, :]))
                e.wait(["sync"], ev, ev.n)
                RBv = carve(32 * 1024, [NCH, SEG], BF16)
                rr(lambda ro: ld_own(ro, sh_r, RBv), "sync")
                e.wait(["scalar"], qs, qs.n)
                for c in range(NCH):
                    es.inc(e.S.activation(out=RBv[:, c, :], in_=RBv[:, c, :],
                                          func=AF.Silu))
                e.wait(["vector"], es, es.n)
                for c in range(NCH):
                    ev.inc(e.V.tensor_mul(out=GAT[:, sego, c, :], in0=YM[:, c, :],
                                          in1=RBv[:, c, :]))
                ev.inc(e.V.engine_nop())
            e.wait(["tensor"], ev, ev.n)

            XSOWN = carve(0, [NCH, SEG], F32)
            X1OWN = carve(32 * 1024, [NCH, SEG], F32)
            WOB = carve(96 * 1024, [2, NCH, SEG], BF16)

            def ld_own2(role, src, dst, sego):
                c0 = own_off(role) + sego * SEG
                qs.dma(e.Y.dma_start(out=dst,
                                     in_=src.rearrange("(c p) t -> p c t", p=128)
                                     [:, :, c0:c0 + SEG]))

            wo_mark = {}
            for sego in range(2):
                e.wait(["sync"], ev, ev.n)
                e.wait(["sync"], qs, qs.n)
                rr(lambda ro: ld_own2(ro, sh_xs, XSOWN, sego), "sync")
                rr(lambda ro: ld_own2(ro, sh_x1, X1OWN, sego), "sync")
                e.wait(["vector"], qs, qs.n)
                for og in range(4):
                    bo = og % 2
                    gk = sego * 4 + og
                    e.wait(["gpsimd"], mm, wo_mark.get(gk - 2, 0))
                    qi.dma(e.G.dma_start(
                        out=WOB[:, bo, :, :],
                        in_=w_out_t.rearrange("(c p) m -> p c m", p=128)
                        [:, :, og * SEG:(og + 1) * SEG]))
                    e.wait(["tensor"], qi, qi.n)
                    for ob4 in range(4):
                        ob = og * 4 + ob4
                        a = ob4 % 2
                        e.wait(["tensor"], ev, ev.n)
                        for c in range(NCH):
                            ins = e.T.matmul(ps[:, a, :],
                                             WOB[:, bo, c, ob4 * 128:(ob4 + 1) * 128],
                                             GAT[:, sego, c, :],
                                             start=(c == 0), stop=(c == NCH - 1))
                        mm.inc(ins)
                        e.wait(["vector"], mm, mm.n)
                        ev.inc(e.V.tensor_add(out=X1OWN[:, ob, :], in0=X1OWN[:, ob, :],
                                              in1=ps[:, a, :]))
                        ev.inc(e.V.tensor_add(out=X1OWN[:, ob, :], in0=X1OWN[:, ob, :],
                                              in1=XSOWN[:, ob, :]))
                    wo_mark[gk] = mm.n
                ev.inc(e.V.engine_nop())
                e.wait(["sync"], ev, ev.n)

                def st_x2(role, sego=sego):
                    c0 = own_off(role) + sego * SEG
                    qs.dma(e.Y.dma_start(
                        out=sh_x1.rearrange("(c p) t -> p c t", p=128)[:, :, c0:c0 + SEG],
                        in_=X1OWN))
                rr(st_x2, "sync")
                e.wait(["vector"], qs, qs.n)
            join()
            dump(9, sh_x1.rearrange("(c p) t -> p c t", p=128)[:, :, 0:SEG])

            XN2 = carve(0, [NCH, 2 * SEG], BF16)

            def x2_own_ap(role, seg):
                c0 = own_off(role) + seg * SEG
                return sh_x1.rearrange("(c p) t -> p c t", p=128)[:, :, c0:c0 + SEG]

            for seg in range(2):
                def ld(seg=seg):
                    rr(lambda ro: qs.dma(e.Y.dma_start(out=XSEGL,
                                                       in_=x2_own_ap(ro, seg))),
                       "sync")
                ln_seg2(ld, XN2, seg * SEG)
            join()
            dump(10, XN2[:, :, 0:SEG])

            ffn_gate_up(XN2, OWN)
            join()
            dump(11, inter_sc.rearrange("(c p) m -> p c m", p=128)[:, 0:16, 0:SEG])

            PARTD2 = carve(96 * 1024, [NCH, SEG], F32)

            def ffn2_xsrc(seg):
                rr(lambda ro: qs.dma(e.Y.dma_start(out=PARTD2,
                                                   in_=x2_own_ap(ro, seg))),
                   "sync")

            def ffn2_store(seg):
                qs.dma(e.Y.dma_start(
                    out=out_t.rearrange("(c p) t -> p c t", p=128)
                    [:, :, seg * SEG:(seg + 1) * SEG],
                    in_=PARTD2))

            ffn_down(OWN, ffn2_xsrc, ffn2_store)
            e.wait(ALL, qs, qs.n)
            e.wait(["gpsimd"], qg, qg.n)

        _sem_final = {}

        def section(eng_obj, name):
            sems = [Sem(h) for h in (qs_h, qg_h, mm_h, ev_h, es_h, ph_h, fg_h,
                                     qw_h, qi_h, ep_h)]
            if name == "gpsimd":
                sems[1].dma(eng_obj.dma_start(out=s_role[:], in_=role_in[:]))
            else:
                sems[1].n += 16
            eng_obj.wait_ge(qg_h, 16)
            emit_body(E(nc, name), sems)
            _sem_final[name] = tuple(s.n for s in sems)

        nc._sem_final = _sem_final

        @blk.tensor
        def _(t):
            section(t, "tensor")

        @blk.vector
        def _(v):
            section(v, "vector")

        @blk.scalar
        def _(s):
            section(s, "scalar")

        @blk.gpsimd
        def _(g):
            section(g, "gpsimd")

        @blk.sync
        def _(s):
            section(s, "sync")

    return nc



def kernel(**inputs):
    return _run(inputs, debug=False)


def _run(inputs, debug=False):
    P = np.concatenate([np.arange(0, DM, 2), np.arange(1, DM, 2)])
    Q = np.concatenate([np.arange(0, MLP, 2), np.arange(1, MLP, 2)])

    x = np.asarray(inputs["L0_attn_x"])
    w_in = np.asarray(inputs["L0_attn_W_in"])
    conv_w = np.asarray(inputs["L0_attn_conv_w"])
    conv_b = np.asarray(inputs["L0_attn_conv_b"])
    w_x = np.asarray(inputs["L0_attn_W_x"])
    w_dt = np.asarray(inputs["L0_attn_W_dt"])
    b_dt = np.asarray(inputs["L0_attn_b_dt"])
    a_log = np.asarray(inputs["L0_attn_A_log"])
    d_in = np.asarray(inputs["L0_attn_D"])
    w_out = np.asarray(inputs["L0_attn_W_out"])
    g_pk = np.asarray(inputs["L0_ffn_gate_W_packed"])
    g_sc = np.asarray(inputs["L0_ffn_gate_scales"])
    u_pk = np.asarray(inputs["L0_ffn_up_W_packed"])
    u_sc = np.asarray(inputs["L0_ffn_up_scales"])
    d_pk = np.asarray(inputs["L0_ffn_down_W_packed"])
    d_sc = np.asarray(inputs["L0_ffn_down_scales"])

    asc = np.ascontiguousarray

    def pack_u8_T(pk_i32, row_perm):
        b = pk_i32[row_perm].view(np.uint8)[:, ::4]
        return asc(b.T)

    gate_pkT = pack_u8_T(g_pk, Q)
    up_pkT = pack_u8_T(u_pk, Q)
    down_pkT = pack_u8_T(d_pk, P)
    gsT = asc(g_sc[Q].T.astype(np.float32))
    usT = asc(u_sc[Q].T.astype(np.float32))
    dsT = asc(d_sc[P].T.astype(np.float32))

    w_in_p = asc(w_in[P])
    conv_wT = asc(conv_w.transpose(2, 1, 0))
    w_out_p = asc(w_out[:, P])

    def r128(v):
        return asc(v.reshape(NCH, 128).T.astype(np.float32))

    conv_b_r = r128(conv_b)
    b_dt_r = r128(b_dt)
    d_r = r128(d_in)

    consts = np.zeros((128, 32), dtype=np.float32)
    for ci, v in enumerate(CONST_VALS):
        consts[:, ci] = v
    iota = np.arange(64, dtype=np.int32).reshape(1, 64)
    ones_col = np.full((128, 1), 1.0 / DM, dtype=ml_dtypes.bfloat16)
    ones_row = np.ones((1, 128), dtype=ml_dtypes.bfloat16)
    ones_f32 = np.ones((128, SEG), dtype=np.float32)

    in_maps = []
    for c in range(8):
        pair = c // 2
        parity = c % 2
        batch = pair // 2
        island = pair % 2
        role = 2 * island + parity
        xb = x[batch].T[P]
        if island == 0:
            xs_ = xb[:, parity * PROD:(parity + 1) * PROD] if parity == 0 \
                else np.zeros((DM, PROD), dtype=np.float32)
        else:
            xs_ = xb[:, parity * PROD:(parity + 1) * PROD]
        a_half = a_log[:, parity * 8:(parity + 1) * 8]
        a_log_r = asc(a_half.reshape(NCH, 128, 8).transpose(1, 0, 2)
                      .reshape(128, NCH * 8).astype(np.float32))
        in_maps.append({
            "x_prod": asc(xs_.astype(np.float32)),
            "w_in": w_in_p, "conv_wT": conv_wT, "conv_b_r": conv_b_r,
            "w_x": asc(w_x.astype(np.float32)), "w_dt": asc(w_dt.astype(np.float32)),
            "b_dt_r": b_dt_r, "a_log_r": a_log_r, "d_r": d_r, "w_out": w_out_p,
            "gate_pkT": gate_pkT, "up_pkT": up_pkT, "down_pkT": down_pkT,
            "gsT": gsT, "usT": usT, "dsT": dsT,
            "kconsts": consts, "iota": iota, "ones_col": ones_col,
            "ones_row": ones_row, "ones_f32": ones_f32,
            "role": np.array([[role, parity, island] + [0] * 13, ], dtype=np.int32),
        })

    nc = build_program(debug=debug)
    res = run_bass_kernel_spmd(nc, in_maps, core_ids=list(range(8)))

    if debug:
        globals()["_dbg_res"] = [np.asarray(r["dbg"]) for r in res.results]

    invP = np.argsort(P)
    out = np.empty((2, LSEQ, DM), dtype=np.float32)
    for c in range(8):
        pair = c // 2
        parity = c % 2
        batch = pair // 2
        island = pair % 2
        t0 = 2048 * island + 1024 * parity
        o = np.asarray(res.results[c]["out"])
        out[batch, t0:t0 + OWN, :] = o[invP].T
    return out


# version 09
# speedup vs baseline: 1.7964x; 1.0804x over previous
import numpy as np
import ml_dtypes
import concourse.bass as bass
import concourse.mybir as mybir
from concourse.bass_utils import run_bass_kernel_spmd

F32 = mybir.dt.float32
BF16 = mybir.dt.bfloat16
I32 = mybir.dt.int32
I16 = mybir.dt.int16
U8 = mybir.dt.uint8
AF = mybir.ActivationFunctionType
ALU = mybir.AluOpType

DM = 2048
DIN = 2048
MLP = 16384
LSEQ = 4096
DTR = 128
KCONV = 4
PROD = 2048
OWN = 1024
SPAN = 4096
SEG = 512
NSEG = PROD // SEG
NCH = DM // 128
NMT = MLP // 128
EPS = 1e-5
GELU_C = 0.7978845608028654

CONST_VALS = [129.53125, -130.0, -4.0, -6.0, 0.0625, 0.125, -1.0, 1.0, 0.5,
              -0.5, 2.0, -2.0, -16.0, -8.0, 8.0, 64.0, 16128.0, 0.0, EPS,
              1.0 / DM, GELU_C, 0.044715]

DBG = False
SKIP_P0 = False

FLG_DEC = 0
FLG_XZ = 1
FLG_PRESCAN = 2
FLG_Y = 3
FLG_X1 = 4
NFLAG = 5


class NullIns:
    def then_inc(self, *a, **k):
        return self

    def __enter__(self):
        return self

    def __exit__(self, *a):
        return False


class NullEng:
    def __getattr__(self, name):
        return lambda *a, **k: NullIns()


class Sem:
    def __init__(self, handle):
        self.h = handle
        self.n = 0

    def inc(self, ins, k=1):
        ins.then_inc(self.h, k)
        self.n += k
        return self.n

    def dma(self, ins):
        return self.inc(ins, 16)


class E:

    ENGS = ("tensor", "vector", "scalar", "gpsimd", "sync")

    def __init__(self, nc, active):
        self.nc = nc
        self.active = active
        self._null = NullEng()

    def on(self, name):
        return getattr(self.nc, name) if name == self.active else self._null

    @property
    def T(self):
        return self.on("tensor")

    @property
    def V(self):
        return self.on("vector")

    @property
    def S(self):
        return self.on("scalar")

    @property
    def G(self):
        return self.on("gpsimd")

    @property
    def Y(self):
        return self.on("sync")

    def wait(self, names, sem, val):
        for nm in names:
            self.on(nm).wait_ge(sem.h, val)


def build_program(debug=DBG):
    nc = bass.Bass("TRN2", target_bir_lowering=False)

    def ein(name, shape, dt):
        return nc.dram_tensor(name, shape, dt, kind="ExternalInput").ap()

    x_in = ein("x_prod", [DM, PROD], F32)
    w_in_t = ein("w_in", [DM, 2 * DIN], F32)
    conv_wT = ein("conv_wT", [KCONV, DIN, DIN], F32)
    conv_b_r = ein("conv_b_r", [128, NCH], F32)
    w_x_t = ein("w_x", [DIN, DTR + 2 * 16], F32)
    w_dt_t = ein("w_dt", [DTR, DIN], F32)
    b_dt_r = ein("b_dt_r", [128, NCH], F32)
    a_log_r = ein("a_log_r", [128, NCH * 8], F32)
    d_r = ein("d_r", [128, NCH], F32)
    w_out_t = ein("w_out", [DIN, DM], F32)
    gate_pkT = ein("gate_pkT", [DM // 2, MLP], U8)
    up_pkT = ein("up_pkT", [DM // 2, MLP], U8)
    down_pkT = ein("down_pkT", [MLP // 2, DM], U8)
    gsT = ein("gsT", [DM // 32, MLP], F32)
    usT = ein("usT", [DM // 32, MLP], F32)
    dsT = ein("dsT", [MLP // 32, DM], F32)
    consts_in = ein("kconsts", [128, 32], F32)
    iota_in = ein("iota", [1, 64], I32)
    ones_col_in = ein("ones_col", [128, 1], BF16)
    ones_row_in = ein("ones_row", [1, 128], BF16)
    ones_f32_in = ein("ones_f32", [128, SEG], F32)
    role_in = ein("role", [1, 16], I32)

    out_t = nc.dram_tensor("out", [DM, OWN], F32, kind="ExternalOutput").ap()
    dbg_t = nc.dram_tensor("dbg", [DM, 20 * SEG], F32,
                           kind="ExternalOutput").ap() if debug else None

    def sh(name, shape, dt):
        return nc.dram_tensor(name, shape, dt, kind="Internal",
                              addr_space="Shared").ap()

    sh_wg = sh("sh_wg", [DM, MLP], BF16)
    sh_wu = sh("sh_wu", [DM, MLP], BF16)
    sh_wd = sh("sh_wd", [MLP, DM], BF16)
    sh_x1 = sh("sh_x1", [DM, SPAN], F32)
    sh_xc = sh("sh_xc", [DIN, SPAN], BF16)
    sh_r = sh("sh_r", [DIN, SPAN], BF16)
    sh_xs = sh("sh_xs", [DIN, SPAN], F32)
    sh_u = sh("sh_u", [DIN, SPAN], BF16)
    sh_delta = sh("sh_delta", [DIN, SPAN], F32)
    sh_bc = sh("sh_bc", [32, SPAN], F32)
    sh_y0 = sh("sh_y0", [DIN, SPAN], F32)
    sh_y1 = sh("sh_y1", [DIN, SPAN], F32)
    sh_flags = nc.dram_tensor("sh_flags", [2 * NFLAG, 16], I32, kind="Internal",
                              addr_space="Shared").ap()
    inter_sc = nc.dram_tensor("inter_sc", [MLP, PROD], BF16, kind="Internal").ap()

    ARENA_B = 149 * 1024

    import contextlib
    with contextlib.ExitStack() as _stk:
        def _e(cm):
            return _stk.enter_context(cm)
        arena = _e(nc.sbuf_tensor("arena", [128, ARENA_B], U8))
        s_consts = _e(nc.sbuf_tensor("s_consts", [128, 32], F32))
        s_role = _e(nc.sbuf_tensor("s_role", [1, 16], I32))
        s_iota = _e(nc.sbuf_tensor("s_iota", [1, 64], I32))
        s_onec = _e(nc.sbuf_tensor("s_onec", [128, 1], BF16))
        s_oner = _e(nc.sbuf_tensor("s_oner", [1, 128], BF16))
        s_A = _e(nc.sbuf_tensor("s_A", [128, NCH * 8], F32))
        s_alog = _e(nc.sbuf_tensor("s_alog", [128, NCH * 8], F32))
        s_D = _e(nc.sbuf_tensor("s_D", [128, NCH], F32))
        s_cb = _e(nc.sbuf_tensor("s_cb", [128, NCH], F32))
        s_bdt = _e(nc.sbuf_tensor("s_bdt", [128, NCH], F32))
        s_state = _e(nc.sbuf_tensor("s_state", [128, NCH * 8], F32))
        s_rows = _e(nc.sbuf_tensor("s_rows", [1, 8 * SEG], F32))
        s_bcr = _e(nc.sbuf_tensor("s_bcr", [32, SEG], F32))
        s_wxw = _e(nc.sbuf_tensor("s_wxw", [128, NCH * 160], BF16))
        s_wdtw = _e(nc.sbuf_tensor("s_wdtw", [128, NCH * 128], BF16))
        ps = _e(nc.psum_tensor("ps", [128, 8, SEG], F32))
        qs_h = _e(nc.semaphore("qs"))
        qg_h = _e(nc.semaphore("qg"))
        mm_h = _e(nc.semaphore("mm"))
        ev_h = _e(nc.semaphore("ev"))
        es_h = _e(nc.semaphore("es"))
        ph_h = _e(nc.semaphore("ph"))
        fg_h = _e(nc.semaphore("fg"))
        qw_h = _e(nc.semaphore("qw"))
        qi_h = _e(nc.semaphore("qi"))
        ep_h = _e(nc.semaphore("ep"))
        qx_h = _e(nc.semaphore("qx"))
        blk = _e(nc.Block())

        for ci, v in enumerate(CONST_VALS):
            nc.const_aps.register(F32, float(v), s_consts[:, ci:ci + 1])

        def carve(off, shape, dt):
            esz = {F32: 4, BF16: 2, I32: 4, U8: 1, I16: 2}[dt]
            nb = int(np.prod(shape)) * esz
            assert off % 512 == 0 and off + nb <= ARENA_B, (off, shape, nb)
            v = arena.ap()[:, off:off + nb].bitcast(dt)
            if len(shape) == 1:
                return v
            names = " ".join(f"d{i}" for i in range(len(shape)))
            v = v.rearrange(f"p ({names}) -> p {names}",
                            **{f"d{i}": s for i, s in enumerate(shape)})
            return v

        def emit_body(e: E, sems):
            qs, qg, mm, ev, es, ph, fg, qw, qi, ep, qx = sems
            ALL = list(E.ENGS)

            aeng = e.on(e.active)
            par_reg = aeng.to_reg(aeng.value_load(s_role[0:1, 1:2]))
            isl_reg = aeng.to_reg(aeng.value_load(s_role[0:1, 2:3]))
            poll_c = aeng.alloc_register()
            poll_f = aeng.alloc_register()

            all_sems = sems

            def _snap():
                return [s.n for s in all_sems]

            def _restore(v):
                for s, x in zip(all_sems, v):
                    s.n = x

            def pp(fn, eng_name):
                en = e.on(eng_name)
                if eng_name != e.active:
                    fn(0)
                    return
                s0 = _snap()
                with en.If_eq(par_reg, 0):
                    fn(0)
                s1 = _snap()
                _restore(s0)
                with en.Else():
                    fn(1)
                assert _snap() == s1, "pp variants count differently"

            def rr(fn, eng_name):
                en = e.on(eng_name)
                if eng_name != e.active:
                    fn(0)
                    return
                s0 = _snap()
                with en.If_eq(isl_reg, 0):
                    with en.If_eq(par_reg, 0):
                        fn(0)
                    with en.Else():
                        _restore(s0)
                        fn(1)
                s1 = _snap()
                with en.Else():
                    with en.If_eq(par_reg, 0):
                        _restore(s0)
                        fn(2)
                    with en.Else():
                        _restore(s0)
                        fn(3)
                assert _snap() == s1, "rr variants count differently"

            def flag_set(fid, value):
                def go(p):
                    qg.dma(e.G.dma_start(
                        out=sh_flags[2 * fid + p:2 * fid + p + 1, 0:1],
                        in_=s_iota[0:1, value:value + 1]))
                pp(go, "gpsimd")

            def flag_poll(fid, value):
                def go(p):
                    g = e.G
                    row = 2 * fid + (1 - p)
                    def cond():
                        g.reg_load(poll_f, sh_flags[row:row + 1, 0:1])
                        g.reg_alu(poll_c, poll_f, value, ALU.is_lt)
                        return poll_c
                    with g.While(cond):
                        g.engine_nop()
                pp(go, "gpsimd")
                return fg.inc(e.G.engine_nop())

            def join():
                for en in ALL:
                    ph.inc(e.on(en).nop())
                t = ph.n
                for en in ALL:
                    e.on(en).wait_ge(ph.h, t)

            def dump(slot, src_ap):
                if not debug:
                    return
                qg.dma(e.G.dma_start(
                    out=dbg_t.rearrange("(c p) t -> p c t", p=128)
                    [:, :, slot * SEG:(slot + 1) * SEG], in_=src_ap))

            qs.dma(e.Y.dma_start(out=s_consts[:], in_=consts_in[:]))
            qs.dma(e.Y.dma_start(out=s_iota[:], in_=iota_in[:]))
            qs.dma(e.Y.dma_start(out=s_onec[:], in_=ones_col_in[:]))
            qs.dma(e.Y.dma_start(out=s_oner[:], in_=ones_row_in[:]))
            qs.dma(e.Y.dma_start(out=s_D[:], in_=d_r[:]))
            qs.dma(e.Y.dma_start(out=s_cb[:], in_=conv_b_r[:]))
            qs.dma(e.Y.dma_start(out=s_bdt[:], in_=b_dt_r[:]))
            qs.dma(e.Y.dma_start(out=s_alog[:], in_=a_log_r[:]))
            e.wait(ALL, qs, qs.n)
            es.inc(e.S.activation(out=s_A[:], in_=s_alog[:], func=AF.Exp))
            e.wait(["vector"], es, es.n)
            ev.inc(e.V.tensor_scalar(out=s_A[:], in0=s_A[:], scalar1=-1.0,
                                     scalar2=None, op0=ALU.mult))
            e.wait(ALL, ev, ev.n)

            MC = 2048
            t_pku8 = carve(106 * 1024, [MC], U8)
            t_sc = carve(108 * 1024, [MC], F32)
            t_pk = carve(116 * 1024, [MC], BF16)
            t_rl = carve(116 * 1024, [MC], BF16)
            t_hi = carve(120 * 1024, [MC], BF16)
            t_lo = carve(124 * 1024, [MC], BF16)
            t_sg = carve(128 * 1024, [MC], BF16)
            t_i16 = carve(132 * 1024, [MC], I16)
            t_o1 = carve(136 * 1024, [MC], BF16)
            t_o2 = carve(140 * 1024, [MC], BF16)
            t_scb = carve(144 * 1024, [MC], BF16)

            def nib_decode(vin, vout):
                ev.inc(e.V.tensor_scalar(out=t_sg, in0=vin, scalar1=0.125,
                                         scalar2=129.53125, op0=ALU.mult, op1=ALU.add))
                ev.inc(e.V.tensor_scalar(out=t_sg, in0=t_sg, scalar1=1.0,
                                         scalar2=-130.0, op0=ALU.mult, op1=ALU.add))
                ev.inc(e.V.scalar_tensor_tensor(out=vout, in0=t_sg, scalar=-8.0,
                                                in1=vin, op0=ALU.mult, op1=ALU.add))
                ev.inc(e.V.tensor_scalar(out=t_i16, in0=vout, scalar1=64.0,
                                         scalar2=16128.0, op0=ALU.mult, op1=ALU.add))
                fb = t_i16.bitcast(BF16)
                ev.inc(e.V.tensor_scalar(out=t_rl, in0=fb, scalar1=-1.0,
                                         scalar2=1.0, op0=ALU.mult, op1=ALU.add))
                ev.inc(e.V.tensor_relu(out=t_rl, in_=t_rl))
                ev.inc(e.V.tensor_sub(out=vout, in0=fb, in1=t_rl))
                ev.inc(e.V.tensor_scalar(out=t_sg, in0=t_sg, scalar1=-2.0,
                                         scalar2=1.0, op0=ALU.mult, op1=ALU.add))
                ev.inc(e.V.tensor_mul(out=vout, in0=vout, in1=t_sg))
                ev.inc(e.V.tensor_mul(out=vout, in0=vout, in1=t_scb))

            dec_cast_mark = [0]

            def dec_tile(pk_src_fn, sc_src_fn, store_fn):
                e.wait(["sync"], ev, dec_cast_mark[0])
                pp(lambda p: qs.dma(e.Y.dma_start(out=t_pku8, in_=pk_src_fn(p))),
                   "sync")
                pp(lambda p: qs.dma(e.Y.dma_start(out=t_sc, in_=sc_src_fn(p))),
                   "sync")
                e.wait(["vector"], qs, qs.n)
                ev.inc(e.V.tensor_copy(out=t_pk, in_=t_pku8))
                ev.inc(e.V.tensor_copy(out=t_scb, in_=t_sc))
                dec_cast_mark[0] = ev.n
                ev.inc(e.V.tensor_scalar(out=t_hi, in0=t_pk, scalar1=0.0625,
                                         scalar2=129.53125, op0=ALU.mult,
                                         op1=ALU.add))
                ev.inc(e.V.tensor_scalar(out=t_hi, in0=t_hi, scalar1=1.0,
                                         scalar2=-130.0, op0=ALU.mult,
                                         op1=ALU.add))
                ev.inc(e.V.scalar_tensor_tensor(out=t_lo, in0=t_hi,
                                                scalar=-16.0, in1=t_pk,
                                                op0=ALU.mult, op1=ALU.add))
                nib_decode(t_lo, t_o1)
                nib_decode(t_hi, t_o2)
                e.wait(["sync"], ev, ev.n)
                pp(store_fn, "sync")

            dec_store_mark = {}

            def dec_chunk_gu(k):
                m0 = k * MC
                for jt in range(8):
                    def pk_src(p, jt=jt, m0=m0):
                        pk = gate_pkT if p == 0 else up_pkT
                        return pk[jt * 128:(jt + 1) * 128, m0:m0 + MC]

                    def sc_src(p, jt=jt, m0=m0):
                        sc = gsT if p == 0 else usT
                        return sc[jt * 8:jt * 8 + 8, m0:m0 + MC][:, None, :]\
                            .broadcast_to((8, 16, MC))

                    def store(p, jt=jt, m0=m0):
                        wdst = sh_wg if p == 0 else sh_wu
                        qs.dma(e.Y.dma_start(
                            out=wdst[jt * 128:(jt + 1) * 128, m0:m0 + MC],
                            in_=t_o1))
                        qs.dma(e.Y.dma_start(
                            out=wdst[DM // 2 + jt * 128:DM // 2 + (jt + 1) * 128,
                                     m0:m0 + MC],
                            in_=t_o2))
                    dec_tile(pk_src, sc_src, store)
                dec_store_mark[k] = qs.n

            def dec_down():
                for jt in range(32):
                    def pk_src(p, jt=jt):
                        jb = p * (MLP // 4)
                        return down_pkT[jb + jt * 128:jb + (jt + 1) * 128, 0:MC]

                    def sc_src(p, jt=jt):
                        r0 = p * (MLP // 4) // 16 + jt * 8
                        return dsT[r0:r0 + 8, 0:MC][:, None, :]\
                            .broadcast_to((8, 16, MC))

                    def store(p, jt=jt):
                        jb = p * (MLP // 4)
                        qs.dma(e.Y.dma_start(
                            out=sh_wd[jb + jt * 128:jb + (jt + 1) * 128, 0:MC],
                            in_=t_o1))
                        qs.dma(e.Y.dma_start(
                            out=sh_wd[MLP // 2 + jb + jt * 128:
                                      MLP // 2 + jb + (jt + 1) * 128, 0:MC],
                            in_=t_o2))
                    dec_tile(pk_src, sc_src, store)
                dec_store_mark[8] = qs.n

            XSEGL = carve(64 * 1024, [NCH, SEG], F32)
            XBF = carve(96 * 1024, [NCH, SEG], BF16)
            SQBF = carve(112 * 1024, [NCH, SEG], BF16)

            def ln_seg2(src_emit, xn_ap, dst_off):
                e.wait(["sync"], ev, ev.n)
                e.wait(["scalar"], ev, ev.n)
                e.wait(["scalar"], mm, mm.n)
                src_emit()
                e.wait(["scalar", "vector"], qs, qs.n)
                for c in range(NCH):
                    es.inc(e.S.activation(out=XBF[:, c, :], in_=XSEGL[:, c, :],
                                          func=AF.Identity))
                    es.inc(e.S.activation(out=SQBF[:, c, :], in_=XSEGL[:, c, :],
                                          func=AF.Square))
                e.wait(["tensor"], es, es.n)
                for c in range(NCH):
                    e.T.matmul(ps[0:1, 6, :], s_onec[:, 0:1], XBF[:, c, :],
                               start=(c == 0), stop=(c == NCH - 1))
                for c in range(NCH):
                    ins = e.T.matmul(ps[0:1, 7, :], s_onec[:, 0:1], SQBF[:, c, :],
                                     start=(c == 0), stop=(c == NCH - 1))
                mm.inc(ins)
                e.wait(["vector"], mm, mm.n)
                ev.inc(e.V.tensor_copy(out=s_rows[0:1, 0:SEG], in_=ps[0:1, 6, :]))
                ev.inc(e.V.scalar_tensor_tensor(out=s_rows[0:1, SEG:2 * SEG],
                                                in0=s_rows[0:1, 0:SEG], scalar=-1.0,
                                                in1=s_rows[0:1, 0:SEG],
                                                op0=ALU.mult, op1=ALU.mult))
                ev.inc(e.V.tensor_add(out=s_rows[0:1, SEG:2 * SEG],
                                      in0=s_rows[0:1, SEG:2 * SEG], in1=ps[0:1, 7, :]))
                e.wait(["scalar"], ev, ev.n)
                es.inc(e.S.activation(out=s_rows[0:1, 2 * SEG:3 * SEG],
                                      in_=s_rows[0:1, SEG:2 * SEG], func=AF.Sqrt,
                                      bias=EPS))
                e.wait(["vector"], es, es.n)
                ev.inc(e.V.reciprocal(out=s_rows[0:1, 2 * SEG:3 * SEG],
                                      in_=s_rows[0:1, 2 * SEG:3 * SEG]))
                ev.inc(e.V.scalar_tensor_tensor(out=s_rows[0:1, 3 * SEG:4 * SEG],
                                                in0=s_rows[0:1, 0:SEG], scalar=-1.0,
                                                in1=s_rows[0:1, 2 * SEG:3 * SEG],
                                                op0=ALU.mult, op1=ALU.mult))
                e.wait(["tensor"], ev, ev.n)
                e.wait(["scalar"], ev, ev.n)
                es.inc(e.S.activation(out=s_rows[0:1, 4 * SEG:5 * SEG].bitcast(BF16)[0:1, 0:SEG],
                                      in_=s_rows[0:1, 2 * SEG:3 * SEG], func=AF.Identity))
                es.inc(e.S.activation(out=s_rows[0:1, 4 * SEG:5 * SEG].bitcast(BF16)[0:1, SEG:2 * SEG],
                                      in_=s_rows[0:1, 3 * SEG:4 * SEG], func=AF.Identity))
                e.wait(["tensor"], es, es.n)
                rows_bf = s_rows[0:1, 4 * SEG:5 * SEG].bitcast(BF16)
                e.T.matmul(ps[:, 6, :], s_oner[0:1, :], rows_bf[0:1, 0:SEG],
                           start=True, stop=True)
                ins = e.T.matmul(ps[:, 7, :], s_oner[0:1, :], rows_bf[0:1, SEG:2 * SEG],
                                 start=True, stop=True)
                mm.inc(ins)
                e.wait(["vector"], mm, mm.n)
                for c in range(NCH):
                    ev.inc(e.V.tensor_mul(out=XBF[:, c, :], in0=XSEGL[:, c, :],
                                          in1=ps[:, 6, :]))
                    ev.inc(e.V.tensor_add(out=xn_ap[:, c, dst_off:dst_off + SEG],
                                          in0=XBF[:, c, :], in1=ps[:, 7, :]))
                ev.inc(e.V.engine_nop())
                e.wait(["tensor"], ev, ev.n)

            MGW = 256

            def ffn_gate_up(xn_ap, tspan, interleave=False):
                WGB = carve(64 * 1024, [2, NCH, MGW], BF16)
                WUB = carve(80 * 1024, [2, NCH, MGW], BF16)
                GEL = carve(96 * 1024, [2, SEG], F32)
                UPS = carve(100 * 1024, [2, SEG], BF16)
                INTB = carve(102 * 1024, [4, SEG], BF16)
                NSGt = tspan // SEG
                gmark = {}
                gel_mark = [0, 0]
                pmarkq = {}
                storeq = {}
                pend = [None]
                git_c = [0]
                for mg in range(MLP // MGW):
                    if interleave and mg % 8 == 0:
                        k = mg // 8
                        e.wait(["gpsimd"], qs, dec_store_mark[k])
                        flag_set(FLG_DEC, k + 1)
                        n = flag_poll(FLG_DEC, k + 1)
                        e.wait(["tensor", "scalar"], fg, n)
                        e.wait(["scalar"], qs, dec_store_mark[k])
                    b = mg % 2
                    if mg >= 2:
                        e.wait(["scalar"], mm, gmark[mg - 2])
                    qw.dma(e.S.dma_start(
                        out=WGB[:, b, :, :],
                        in_=sh_wg.rearrange("(c p) m -> p c m", p=128)
                        [:, :, mg * MGW:(mg + 1) * MGW]))
                    qw.dma(e.S.dma_start(
                        out=WUB[:, b, :, :],
                        in_=sh_wu.rearrange("(c p) m -> p c m", p=128)
                        [:, :, mg * MGW:(mg + 1) * MGW]))
                    e.wait(["tensor"], qw, qw.n)
                    for mt4 in range(MGW // 128):
                        mt = mg * (MGW // 128) + mt4
                        for sg in range(NSGt):
                            a = sg % 2
                            e.wait(["tensor"], es, gel_mark[a])
                            for c in range(NCH):
                                e.T.matmul(ps[:, a, :],
                                           WGB[:, b, c, mt4 * 128:(mt4 + 1) * 128],
                                           xn_ap[:, c, sg * SEG:(sg + 1) * SEG],
                                           start=(c == 0), stop=(c == NCH - 1))
                            for c in range(NCH):
                                ins = e.T.matmul(ps[:, 2 + a, :],
                                                 WUB[:, b, c, mt4 * 128:(mt4 + 1) * 128],
                                                 xn_ap[:, c, sg * SEG:(sg + 1) * SEG],
                                                 start=(c == 0), stop=(c == NCH - 1))
                            mm.inc(ins)
                            git = git_c[0]
                            e.wait(["scalar"], mm, mm.n)
                            e.wait(["scalar"], ep, pmarkq.get(git - 2, 0))
                            es.inc(e.S.activation(out=GEL[:, a, :], in_=ps[:, a, :],
                                                  func=AF.Gelu_apprx_tanh))
                            es.inc(e.S.activation(out=UPS[:, a, :], in_=ps[:, 2 + a, :],
                                                  func=AF.Identity))
                            gel_mark[a] = es.n
                            e.wait(["gpsimd"], es, es.n)
                            e.wait(["gpsimd"], qx, storeq.get(git - 4, 0))
                            ep.inc(e.G.tensor_mul(out=INTB[:, git % 4, :],
                                                  in0=GEL[:, a, :],
                                                  in1=UPS[:, a, :]))
                            pmarkq[git] = ep.n
                            if pend[0] is not None:
                                pg, pmt, psg = pend[0]
                                e.wait(["scalar"], ep, pmarkq[pg])
                                qx.dma(e.S.dma_start(
                                    out=inter_sc[pmt * 128:(pmt + 1) * 128,
                                                 psg * SEG:(psg + 1) * SEG],
                                    in_=INTB[:, pg % 4, :]))
                                storeq[pg] = qx.n
                            pend[0] = (git, mt, sg)
                            git_c[0] = git + 1
                    gmark[mg] = mm.n
                pg, pmt, psg = pend[0]
                e.wait(["scalar"], ep, pmarkq[pg])
                qx.dma(e.S.dma_start(
                    out=inter_sc[pmt * 128:(pmt + 1) * 128,
                                 psg * SEG:(psg + 1) * SEG],
                    in_=INTB[:, pg % 4, :]))
                storeq[pg] = qx.n

            def ffn_down(tspan, xsrc_emit, store_emit):
                IQB = carve(0, [2, 16, SEG], BF16)
                WDS = carve(32 * 1024, [2, 16, 1024], BF16)
                PART = carve(96 * 1024, [NCH, SEG], F32)
                nsg = tspan // SEG
                iq_mark = {}
                slab_mark = {}
                slab_load = {}
                iq_load = {}
                gmarks = {}
                part_store = 0
                e.wait(["sync"], qx, qx.n)
                for seg in range(nsg):
                    e.wait(["sync"], qs, qs.n)
                    xsrc_emit(seg)
                    part_load = qs.n
                    for R in range(8):
                        gi = seg * 8 + R
                        bi = gi % 2
                        e.wait(["sync"], mm, iq_mark.get(gi - 2, 0))
                        qs.dma(e.Y.dma_start(
                            out=IQB[:, bi, :, :],
                            in_=inter_sc.rearrange("(J p) t -> p J t", p=128)
                            [:, R * 16:(R + 1) * 16, seg * SEG:(seg + 1) * SEG]))
                        iq_load[gi] = qs.n
                        for hbg in range(2):
                            si = 2 * gi + hbg
                            ws = si % 2
                            e.wait(["scalar"], mm, slab_mark.get(si - 2, 0))
                            qw.dma(e.S.dma_start(
                                out=WDS[:, ws, :, :],
                                in_=sh_wd.rearrange("(J p) h -> p J h", p=128)
                                [:, R * 16:(R + 1) * 16,
                                 hbg * 1024:(hbg + 1) * 1024]))
                            slab_load[si] = qw.n
                        for hbg in range(2):
                            si = 2 * gi + hbg
                            ws = si % 2
                            e.wait(["tensor"], qs, iq_load[gi])
                            e.wait(["tensor"], qw, slab_load[si])
                            e.wait(["tensor"], ev, ev.n)
                            for hbl in range(8):
                                for J in range(16):
                                    ins = e.T.matmul(ps[:, hbl, :],
                                                     WDS[:, ws, J,
                                                         hbl * 128:(hbl + 1) * 128],
                                                     IQB[:, bi, J, :],
                                                     start=(J == 0), stop=(J == 15))
                                mm.inc(ins)
                            slab_mark[si] = mm.n
                            e.wait(["vector"], mm, mm.n)
                            if gi == seg * 8 and hbg == 0:
                                e.wait(["vector"], qs, part_load)
                            for hbl in range(8):
                                ev.inc(e.V.scalar_tensor_tensor(
                                    out=PART[:, hbg * 8 + hbl, :],
                                    in0=ps[:, hbl, :], scalar=0.5,
                                    in1=PART[:, hbg * 8 + hbl, :],
                                    op0=ALU.mult, op1=ALU.add))
                            ev.inc(e.V.engine_nop())
                        iq_mark[gi] = mm.n
                    e.wait(["sync"], ev, ev.n)
                    store_emit(seg)
                e.wait(["sync"], qs, qs.n)

            XN = carve(0, [NCH, PROD], BF16)

            def x_src(p, seg):
                return x_in.rearrange("(c p) t -> p c t", p=128)[:, :, seg * SEG:(seg + 1) * SEG]

            for seg in range(NSEG):
                def ld(seg=seg):
                    pp(lambda p: qs.dma(e.Y.dma_start(out=XSEGL, in_=x_src(p, seg))),
                       "sync")
                ln_seg2(ld, XN, seg * SEG)
            join()
            dump(0, XN[:, :, 0:SEG])

            for k in range(8):
                dec_chunk_gu(k)
            dec_down()
            ffn_gate_up(XN, PROD, interleave=True)
            e.wait(["gpsimd"], qs, dec_store_mark[8])
            flag_set(FLG_DEC, 9)
            join()
            dump(1, inter_sc.rearrange("(c p) m -> p c m", p=128)[:, 0:16, 0:SEG])
            nD = flag_poll(FLG_DEC, 9)
            e.wait(["tensor", "scalar", "sync"], fg, nD)
            e.wait(["scalar"], qs, dec_store_mark[8])

            PARTD = carve(96 * 1024, [NCH, SEG], F32)

            def x1_prod_ap(p, seg):
                c0 = p * PROD + seg * SEG
                return sh_x1.rearrange("(c p) t -> p c t", p=128)[:, :, c0:c0 + SEG]

            def ffn1_xsrc(seg):
                pp(lambda p: qs.dma(e.Y.dma_start(out=PARTD, in_=x_src(p, seg))),
                   "sync")

            def ffn1_store(seg):
                pp(lambda p: qs.dma(e.Y.dma_start(out=x1_prod_ap(p, seg), in_=PARTD)),
                   "sync")

            ffn_down(PROD, ffn1_xsrc, ffn1_store)
            e.wait(["gpsimd"], qs, qs.n)
            flag_set(FLG_X1, 1)
            join()
            dump(2, sh_x1.rearrange("(c p) t -> p c t", p=128)[:, :, 0:SEG])

            for seg in range(NSEG):
                def ld(seg=seg):
                    pp(lambda p: qs.dma(e.Y.dma_start(out=XSEGL, in_=x1_prod_ap(p, seg))),
                       "sync")
                ln_seg2(ld, XN, seg * SEG)
            join()
            dump(3, XN[:, :, 0:SEG])

            WIB = carve(64 * 1024, [2, NCH, SEG], BF16)
            EVA = carve(96 * 1024, [2, SEG], BF16)
            wi_mark = {}
            eva_store = [0, 0]
            bank_mark = [0, 0]
            for og in range(8):
                bo = og % 2
                e.wait(["gpsimd"], mm, wi_mark.get(og - 2, 0))
                qi.dma(e.G.dma_start(
                    out=WIB[:, bo, :, :],
                    in_=w_in_t.rearrange("(c p) m -> p c m", p=128)
                    [:, :, og * SEG:(og + 1) * SEG]))
                e.wait(["tensor"], qi, qi.n)
                for ob4 in range(4):
                    ob = og * 4 + ob4
                    for seg in range(NSEG):
                        a = (NSEG * ob4 + seg) % 2
                        e.wait(["tensor"], es, bank_mark[a])
                        for c in range(NCH):
                            ins = e.T.matmul(ps[:, a, :],
                                             WIB[:, bo, c, ob4 * 128:(ob4 + 1) * 128],
                                             XN[:, c, seg * SEG:(seg + 1) * SEG],
                                             start=(c == 0), stop=(c == NCH - 1))
                        mm.inc(ins)
                        e.wait(["scalar"], mm, mm.n)
                        e.wait(["scalar"], qs, eva_store[a])
                        es.inc(e.S.activation(out=EVA[:, a, :], in_=ps[:, a, :],
                                              func=AF.Identity))
                        bank_mark[a] = es.n
                        e.wait(["sync"], es, es.n)
                        if ob < NCH:
                            dst = sh_xc
                            rw = ob
                        else:
                            dst = sh_r
                            rw = ob - NCH

                        def store(p, dst=dst, rw=rw, seg=seg, a=a):
                            c0 = p * PROD + seg * SEG
                            qs.dma(e.Y.dma_start(
                                out=dst[rw * 128:(rw + 1) * 128, c0:c0 + SEG],
                                in_=EVA[:, a, :]))
                        pp(store, "sync")
                        eva_store[a] = qs.n
                wi_mark[og] = mm.n
            e.wait(["gpsimd"], qs, qs.n)
            flag_set(FLG_XZ, 1)
            n = flag_poll(FLG_XZ, 1)
            e.wait(ALL, fg, n)
            join()
            dump(4, sh_xc.rearrange("(c p) t -> p c t", p=128)[:, :, 0:SEG])
            dump(12, sh_r.rearrange("(c p) t -> p c t", p=128)[:, :, 0:SEG])

            XCW = carve(0, [NCH, SEG + 4], BF16)
            CWB = carve(17 * 1024, [2, NCH, KCONV, 256], BF16)
            UEV = carve(84 * 1024, [SEG], BF16)
            XSEV = carve(86 * 1024, [SEG], F32)
            cv_mark = {}
            cbank = [0, 0]
            u_store = 0
            for obg in range(8):
                bo = obg % 2
                e.wait(["gpsimd"], mm, cv_mark.get(obg - 2, 0))
                for kk in range(KCONV):
                    qi.dma(e.G.dma_start(
                        out=CWB[:, bo, :, kk, :],
                        in_=conv_wT.rearrange("k (c p) o -> p k c o", p=128)
                        [:, kk, :, obg * 256:(obg + 1) * 256]))
                e.wait(["tensor"], qi, qi.n)
                if debug and obg < 2:
                    e.wait(["gpsimd"], qi, qi.n)
                    dump(14 + 2 * obg, CWB[:, bo, :, 0:2, :])
                    dump(15 + 2 * obg, CWB[:, bo, :, 2:4, :])
                    e.wait(["gpsimd"], qg, qg.n)
                for seg in range(NSEG):
                    e.wait(["vector"], mm, mm.n)
                    e.wait(["sync"], mm, mm.n)
                    ev.inc(e.V.memset(XCW[:, :, 0:4], 0.0))
                    e.wait(["sync"], ev, ev.n)

                    def ldwin(p, seg=seg):
                        c0 = p * PROD + seg * SEG
                        if c0 == 0:
                            qs.dma(e.Y.dma_start(
                                out=XCW[:, :, 4:SEG + 4],
                                in_=sh_xc.rearrange("(c p) t -> p c t", p=128)
                                [:, :, 0:SEG]))
                        else:
                            qs.dma(e.Y.dma_start(
                                out=XCW[:, :, 0:SEG + 4],
                                in_=sh_xc.rearrange("(c p) t -> p c t", p=128)
                                [:, :, c0 - 4:c0 + SEG]))
                    pp(ldwin, "sync")
                    e.wait(["tensor"], qs, qs.n)
                    if debug and obg == 0 and seg == 0:
                        e.wait(["gpsimd"], qs, qs.n)
                        dump(18, XCW[:, :, 4:4 + SEG])
                        e.wait(["gpsimd"], qg, qg.n)
                    for ob2 in range(2):
                        ob = obg * 2 + ob2
                        a = (2 * seg + ob2) % 2
                        e.wait(["tensor"], es, cbank[a])
                        for c in range(NCH):
                            for k in range(KCONV):
                                ins = e.T.matmul(ps[:, a, :],
                                                 CWB[:, bo, c, k,
                                                     ob2 * 128:(ob2 + 1) * 128],
                                                 XCW[:, c, 1 + k:1 + k + SEG],
                                                 start=(c == 0 and k == 0),
                                                 stop=(c == NCH - 1 and k == KCONV - 1))
                        mm.inc(ins)
                        e.wait(["scalar"], mm, mm.n)
                        e.wait(["scalar"], qs, u_store)
                        es.inc(e.S.activation(out=UEV, in_=ps[:, a, :],
                                              func=AF.Silu, bias=s_cb[:, ob:ob + 1]))
                        es.inc(e.S.activation(out=XSEV, in_=ps[:, a, :],
                                              func=AF.Identity, bias=s_cb[:, ob:ob + 1]))
                        cbank[a] = es.n
                        e.wait(["sync"], es, es.n)

                        def store_u(p, ob=ob, seg=seg):
                            c0 = p * PROD + seg * SEG
                            qs.dma(e.Y.dma_start(out=sh_u[ob * 128:(ob + 1) * 128,
                                                          c0:c0 + SEG],
                                                 in_=UEV))
                            qs.dma(e.Y.dma_start(out=sh_xs[ob * 128:(ob + 1) * 128,
                                                           c0:c0 + SEG],
                                                 in_=XSEV))
                        pp(store_u, "sync")
                        u_store = qs.n
                cv_mark[obg] = mm.n
            join()
            dump(5, sh_u.rearrange("(c p) t -> p c t", p=128)[:, :, 0:SEG])
            dump(13, sh_xs.rearrange("(c p) t -> p c t", p=128)[:, :, 0:SEG])

            qg.dma(e.G.dma_start(out=s_wxw.rearrange("p (c f) -> p c f", c=NCH),
                                 in_=w_x_t.rearrange("(c p) f -> p c f", p=128)))
            qg.dma(e.G.dma_start(out=s_wdtw.rearrange("p (c f) -> p c f", c=NCH),
                                 in_=w_dt_t.rearrange("p (c f) -> p c f", c=NCH)))
            e.wait(["tensor"], qg, qg.n)
            UCX = carve(0, [NCH, SEG], BF16)
            DBC = carve(16 * 1024, [SEG], BF16)
            DEL = carve(20 * 1024, [SEG], F32)
            DE2 = carve(24 * 1024, [SEG], F32)
            DE3 = carve(28 * 1024, [SEG], F32)

            for seg in range(NSEG):
                if seg > 0:
                    e.wait(["sync"], mm, mm.n)
                    e.wait(["sync"], es, es.n)

                def ld_u(p, seg=seg):
                    c0 = p * PROD + seg * SEG
                    qs.dma(e.Y.dma_start(out=UCX,
                                         in_=sh_u.rearrange("(c p) t -> p c t", p=128)
                                         [:, :, c0:c0 + SEG]))
                pp(ld_u, "sync")
                e.wait(["tensor"], qs, qs.n)
                if seg > 0:
                    e.wait(["tensor"], ev, ev.n)
                    e.wait(["tensor"], es, es.n)
                    e.wait(["scalar"], mm, mm.n)
                for c in range(NCH):
                    e.T.matmul(ps[:, 6, :], s_wxw.rearrange("p (c f) -> p c f", c=NCH)
                               [:, c, 0:DTR],
                               UCX[:, c, :],
                               start=(c == 0), stop=(c == NCH - 1))
                for c in range(NCH):
                    ins = e.T.matmul(ps[0:32, 7, :], s_wxw.rearrange("p (c f) -> p c f", c=NCH)
                                     [:, c, DTR:DTR + 32],
                                     UCX[:, c, :],
                                     start=(c == 0), stop=(c == NCH - 1))
                mm.inc(ins)
                e.wait(["scalar", "vector"], mm, mm.n)
                es.inc(e.S.activation(out=DBC, in_=ps[:, 6, :], func=AF.Identity))
                ev.inc(e.V.tensor_copy(out=s_bcr[:, :], in_=ps[0:32, 7, :]))
                e.wait(["sync"], ev, ev.n)

                def store_bc(p, seg=seg):
                    c0 = p * PROD + seg * SEG
                    qs.dma(e.Y.dma_start(out=sh_bc[:, c0:c0 + SEG], in_=s_bcr[:, :]))
                pp(store_bc, "sync")
                e.wait(["tensor"], es, es.n)
                e.wait(["tensor"], ev, ev.n)
                for ob in range(NCH):
                    if ob > 0:
                        e.wait(["tensor"], es, es.n)
                    ins = e.T.matmul(ps[:, 7, :],
                                     s_wdtw.rearrange("p (c f) -> p c f", c=NCH)[:, ob, :],
                                     DBC, start=True, stop=True)
                    mm.inc(ins)
                    e.wait(["scalar"], mm, mm.n)
                    es.inc(e.S.activation(out=DEL, in_=ps[:, 7, :], func=AF.Identity,
                                          bias=s_bdt[:, ob:ob + 1]))
                    e.wait(["vector"], es, es.n)
                    ev.inc(e.V.tensor_relu(out=DE2, in_=DEL))
                    e.wait(["scalar"], ev, ev.n)
                    es.inc(e.S.activation(out=DE3, in_=DEL, func=AF.Abs))
                    e.wait(["scalar"], es, es.n)
                    es.inc(e.S.activation(out=DE3, in_=DE3, func=AF.Exp, scale=-1.0))
                    es.inc(e.S.activation(out=DE3, in_=DE3, func=AF.Ln, bias=1.0))
                    e.wait(["vector"], es, es.n)
                    ev.inc(e.V.tensor_add(out=DEL, in0=DE2, in1=DE3))
                    e.wait(["sync"], ev, ev.n)

                    def store_del(p, ob=ob, seg=seg):
                        c0 = p * PROD + seg * SEG
                        qs.dma(e.Y.dma_start(
                            out=sh_delta[ob * 128:(ob + 1) * 128, c0:c0 + SEG],
                            in_=DEL))
                    pp(store_del, "sync")
                    e.wait(["scalar", "tensor"], qs, qs.n)
            e.wait(["gpsimd"], qs, qs.n)
            flag_set(FLG_PRESCAN, 1)
            n = flag_poll(FLG_PRESCAN, 1)
            e.wait(ALL, fg, n)
            join()
            dump(6, sh_delta.rearrange("(c p) t -> p c t", p=128)[:, :, 0:SEG])

            DELC = carve(0, [NCH, SEG], F32)
            UC = carve(32 * 1024, [NCH, SEG], BF16)
            DUC = carve(48 * 1024, [NCH, SEG], F32)
            YC = carve(80 * 1024, [NCH, SEG], F32)
            BCB = carve(112 * 1024, [2, 8, SEG], BF16)
            DAT = carve(128 * 1024, [2, SEG], F32)
            DBT1 = carve(132 * 1024, [3, SEG], F32)
            DBT2 = carve(138 * 1024, [3, SEG], F32)
            HT = carve(144 * 1024, [2, SEG], F32)

            ev.inc(e.V.memset(s_state[:, :], 0.0))
            e.wait(ALL, ev, ev.n)

            scan_mark = {}
            pool_mark = {}
            for tc in range(SPAN // SEG):
                c0 = tc * SEG
                e.wait(["sync"], ev, ev.n)
                e.wait(["sync"], es, es.n)
                e.wait(["sync"], ep, ep.n)
                qs.dma(e.Y.dma_start(out=DELC,
                                     in_=sh_delta.rearrange("(c p) t -> p c t", p=128)
                                     [:, :, c0:c0 + SEG]))
                qs.dma(e.Y.dma_start(out=UC,
                                     in_=sh_u.rearrange("(c p) t -> p c t", p=128)
                                     [:, :, c0:c0 + SEG]))
                e.wait(["gpsimd"], ev, ev.n)

                def ld_bc(p, c0=c0):
                    r0 = p * 8
                    qi.dma(e.G.dma_start(
                        out=BCB[:, 0, :, :],
                        in_=sh_bc[None, r0:r0 + 8, c0:c0 + SEG]
                        .broadcast_to((128, 8, SEG))))
                    qi.dma(e.G.dma_start(
                        out=BCB[:, 1, :, :],
                        in_=sh_bc[None, 16 + r0:16 + r0 + 8, c0:c0 + SEG]
                        .broadcast_to((128, 8, SEG))))
                pp(ld_bc, "gpsimd")
                e.wait(["vector", "scalar"], qs, qs.n)
                e.wait(["vector"], qi, qi.n)
                e.wait(["vector"], ep, ep.n)
                for c in range(NCH):
                    ev.inc(e.V.tensor_mul(out=DUC[:, c, :], in0=DELC[:, c, :],
                                          in1=UC[:, c, :]))

                def yinit(p):
                    for c in range(NCH):
                        if p == 0:
                            ev.inc(e.V.tensor_scalar(out=YC[:, c, :],
                                                     in0=UC[:, c, :],
                                                     scalar1=s_D[:, c:c + 1],
                                                     scalar2=None, op0=ALU.mult))
                        else:
                            ev.inc(e.V.memset(YC[:, c, :], 0.0))
                pp(yinit, "vector")
                ev.inc(e.V.engine_nop())
                yinit_mark = ev.n
                e.wait(["gpsimd"], ev, yinit_mark)
                e.wait(["scalar"], ev, yinit_mark)
                e.wait(["gpsimd"], qi, qi.n)
                NIT = 8 * NCH
                dbt1_mark = {}

                def emit_dbt1(j):
                    if j >= NIT:
                        return
                    nnj, cj = j // NCH, j % NCH
                    e.wait(["gpsimd"], ev, max(yinit_mark,
                                               scan_mark.get(j - 3, 0)))
                    ep.inc(e.G.tensor_mul(out=DBT1[:, j % 3, :],
                                          in0=DUC[:, cj, :],
                                          in1=BCB[:, 0, nnj, :]))
                    dbt1_mark[j] = ep.n

                emit_dbt1(0)
                emit_dbt1(1)
                it = 0
                for nn in range(8):
                    for c in range(NCH):
                        col = c * 8 + nn
                        x = it % 2
                        e.wait(["scalar"], ev, scan_mark.get(it - 2, 0))
                        es.inc(e.S.activation(out=DAT[:, x, :], in_=DELC[:, c, :],
                                              func=AF.Exp,
                                              scale=s_A[:, col:col + 1]))
                        emit_dbt1(it + 2)
                        e.wait(["vector"], es, es.n)
                        e.wait(["vector"], ep, dbt1_mark[it])
                        e.wait(["vector"], ep, pool_mark.get(it - 3, 0))
                        ev.inc(e.V.tensor_tensor_scan(out=HT[:, x, :],
                                                      data0=DAT[:, x, :],
                                                      data1=DBT1[:, it % 3, :],
                                                      initial=s_state[:, col:col + 1],
                                                      op0=ALU.mult, op1=ALU.add))
                        ev.inc(e.V.tensor_copy(out=s_state[:, col:col + 1],
                                               in_=HT[:, x, SEG - 1:SEG]))
                        ev.inc(e.V.tensor_mul(out=DBT2[:, it % 3, :], in0=HT[:, x, :],
                                              in1=BCB[:, 1, nn, :]))
                        scan_mark[it] = ev.n
                        e.wait(["gpsimd"], ev, scan_mark[it])
                        ep.inc(e.G.tensor_add(out=YC[:, c, :], in0=YC[:, c, :],
                                              in1=DBT2[:, it % 3, :]))
                        pool_mark[it] = ep.n
                        it += 1
                e.wait(["sync"], ep, ep.n)
                e.wait(["sync"], ev, ev.n)

                def store_y(p, tc=tc):
                    dst = sh_y0 if p == 0 else sh_y1
                    qs.dma(e.Y.dma_start(
                        out=dst.rearrange("(c p) t -> p c t", p=128)
                        [:, :, tc * SEG:(tc + 1) * SEG], in_=YC))
                pp(store_y, "sync")
                e.wait(["gpsimd"], qs, qs.n)
                flag_set(FLG_Y, tc + 1)
                e.wait(["vector"], qs, qs.n)
            n = flag_poll(FLG_Y, SPAN // SEG)
            e.wait(ALL, fg, n)
            nx1 = flag_poll(FLG_X1, 1)
            e.wait(ALL, fg, nx1)
            join()
            dump(7, sh_y0.rearrange("(c p) t -> p c t", p=128)[:, :, 0:SEG])
            dump(8, sh_y1.rearrange("(c p) t -> p c t", p=128)[:, :, 0:SEG])

            YM = carve(0, [NCH, SEG], F32)
            YP = carve(32 * 1024, [NCH, SEG], F32)
            GAT = carve(64 * 1024, [2, NCH, SEG], BF16)

            def own_off(role):
                return 1024 * role

            for sego in range(2):
                e.wait(["sync"], ev, ev.n)
                e.wait(["sync"], es, es.n)

                def ld_own(role, src, dst, sego=sego):
                    c0 = own_off(role) + sego * SEG
                    qs.dma(e.Y.dma_start(out=dst,
                                         in_=src.rearrange("(c p) t -> p c t", p=128)
                                         [:, :, c0:c0 + SEG]))
                rr(lambda ro: ld_own(ro, sh_y0, YM), "sync")
                rr(lambda ro: ld_own(ro, sh_y1, YP), "sync")
                e.wait(["vector"], qs, qs.n)
                for c in range(NCH):
                    ev.inc(e.V.tensor_add(out=YM[:, c, :], in0=YM[:, c, :],
                                          in1=YP[:, c, :]))
                e.wait(["sync"], ev, ev.n)
                RBv = carve(32 * 1024, [NCH, SEG], BF16)
                rr(lambda ro: ld_own(ro, sh_r, RBv), "sync")
                e.wait(["scalar"], qs, qs.n)
                for c in range(NCH):
                    es.inc(e.S.activation(out=RBv[:, c, :], in_=RBv[:, c, :],
                                          func=AF.Silu))
                e.wait(["vector"], es, es.n)
                for c in range(NCH):
                    ev.inc(e.V.tensor_mul(out=GAT[:, sego, c, :], in0=YM[:, c, :],
                                          in1=RBv[:, c, :]))
                ev.inc(e.V.engine_nop())
            e.wait(["tensor"], ev, ev.n)

            XSOWN = carve(0, [NCH, SEG], F32)
            X1OWN = carve(32 * 1024, [NCH, SEG], F32)
            WOB = carve(96 * 1024, [2, NCH, SEG], BF16)

            def ld_own2(role, src, dst, sego):
                c0 = own_off(role) + sego * SEG
                qs.dma(e.Y.dma_start(out=dst,
                                     in_=src.rearrange("(c p) t -> p c t", p=128)
                                     [:, :, c0:c0 + SEG]))

            wo_mark = {}
            for sego in range(2):
                e.wait(["sync"], ev, ev.n)
                e.wait(["sync"], qs, qs.n)
                rr(lambda ro: ld_own2(ro, sh_xs, XSOWN, sego), "sync")
                rr(lambda ro: ld_own2(ro, sh_x1, X1OWN, sego), "sync")
                e.wait(["vector"], qs, qs.n)
                for og in range(4):
                    bo = og % 2
                    gk = sego * 4 + og
                    e.wait(["gpsimd"], mm, wo_mark.get(gk - 2, 0))
                    qi.dma(e.G.dma_start(
                        out=WOB[:, bo, :, :],
                        in_=w_out_t.rearrange("(c p) m -> p c m", p=128)
                        [:, :, og * SEG:(og + 1) * SEG]))
                    e.wait(["tensor"], qi, qi.n)
                    for ob4 in range(4):
                        ob = og * 4 + ob4
                        a = ob4 % 2
                        e.wait(["tensor"], ev, ev.n)
                        for c in range(NCH):
                            ins = e.T.matmul(ps[:, a, :],
                                             WOB[:, bo, c, ob4 * 128:(ob4 + 1) * 128],
                                             GAT[:, sego, c, :],
                                             start=(c == 0), stop=(c == NCH - 1))
                        mm.inc(ins)
                        e.wait(["vector"], mm, mm.n)
                        ev.inc(e.V.tensor_add(out=X1OWN[:, ob, :], in0=X1OWN[:, ob, :],
                                              in1=ps[:, a, :]))
                        ev.inc(e.V.tensor_add(out=X1OWN[:, ob, :], in0=X1OWN[:, ob, :],
                                              in1=XSOWN[:, ob, :]))
                    wo_mark[gk] = mm.n
                ev.inc(e.V.engine_nop())
                e.wait(["sync"], ev, ev.n)

                def st_x2(role, sego=sego):
                    c0 = own_off(role) + sego * SEG
                    qs.dma(e.Y.dma_start(
                        out=sh_x1.rearrange("(c p) t -> p c t", p=128)[:, :, c0:c0 + SEG],
                        in_=X1OWN))
                rr(st_x2, "sync")
                e.wait(["vector"], qs, qs.n)
            join()
            dump(9, sh_x1.rearrange("(c p) t -> p c t", p=128)[:, :, 0:SEG])

            XN2 = carve(0, [NCH, 2 * SEG], BF16)

            def x2_own_ap(role, seg):
                c0 = own_off(role) + seg * SEG
                return sh_x1.rearrange("(c p) t -> p c t", p=128)[:, :, c0:c0 + SEG]

            for seg in range(2):
                def ld(seg=seg):
                    rr(lambda ro: qs.dma(e.Y.dma_start(out=XSEGL,
                                                       in_=x2_own_ap(ro, seg))),
                       "sync")
                ln_seg2(ld, XN2, seg * SEG)
            join()
            dump(10, XN2[:, :, 0:SEG])

            ffn_gate_up(XN2, OWN)
            join()
            dump(11, inter_sc.rearrange("(c p) m -> p c m", p=128)[:, 0:16, 0:SEG])

            PARTD2 = carve(96 * 1024, [NCH, SEG], F32)

            def ffn2_xsrc(seg):
                rr(lambda ro: qs.dma(e.Y.dma_start(out=PARTD2,
                                                   in_=x2_own_ap(ro, seg))),
                   "sync")

            def ffn2_store(seg):
                qs.dma(e.Y.dma_start(
                    out=out_t.rearrange("(c p) t -> p c t", p=128)
                    [:, :, seg * SEG:(seg + 1) * SEG],
                    in_=PARTD2))

            ffn_down(OWN, ffn2_xsrc, ffn2_store)
            e.wait(ALL, qs, qs.n)
            e.wait(["gpsimd"], qg, qg.n)

        _sem_final = {}

        def section(eng_obj, name):
            sems = [Sem(h) for h in (qs_h, qg_h, mm_h, ev_h, es_h, ph_h, fg_h,
                                     qw_h, qi_h, ep_h, qx_h)]
            if name == "gpsimd":
                sems[1].dma(eng_obj.dma_start(out=s_role[:], in_=role_in[:]))
            else:
                sems[1].n += 16
            eng_obj.wait_ge(qg_h, 16)
            emit_body(E(nc, name), sems)
            _sem_final[name] = tuple(s.n for s in sems)

        nc._sem_final = _sem_final

        @blk.tensor
        def _(t):
            section(t, "tensor")

        @blk.vector
        def _(v):
            section(v, "vector")

        @blk.scalar
        def _(s):
            section(s, "scalar")

        @blk.gpsimd
        def _(g):
            section(g, "gpsimd")

        @blk.sync
        def _(s):
            section(s, "sync")

    return nc



def kernel(**inputs):
    return _run(inputs, debug=False)


def _run(inputs, debug=False):
    P = np.concatenate([np.arange(0, DM, 2), np.arange(1, DM, 2)])
    Q = np.concatenate([np.arange(0, MLP, 2), np.arange(1, MLP, 2)])

    x = np.asarray(inputs["L0_attn_x"])
    w_in = np.asarray(inputs["L0_attn_W_in"])
    conv_w = np.asarray(inputs["L0_attn_conv_w"])
    conv_b = np.asarray(inputs["L0_attn_conv_b"])
    w_x = np.asarray(inputs["L0_attn_W_x"])
    w_dt = np.asarray(inputs["L0_attn_W_dt"])
    b_dt = np.asarray(inputs["L0_attn_b_dt"])
    a_log = np.asarray(inputs["L0_attn_A_log"])
    d_in = np.asarray(inputs["L0_attn_D"])
    w_out = np.asarray(inputs["L0_attn_W_out"])
    g_pk = np.asarray(inputs["L0_ffn_gate_W_packed"])
    g_sc = np.asarray(inputs["L0_ffn_gate_scales"])
    u_pk = np.asarray(inputs["L0_ffn_up_W_packed"])
    u_sc = np.asarray(inputs["L0_ffn_up_scales"])
    d_pk = np.asarray(inputs["L0_ffn_down_W_packed"])
    d_sc = np.asarray(inputs["L0_ffn_down_scales"])

    asc = np.ascontiguousarray

    def pack_u8_T(pk_i32, row_perm):
        b = pk_i32[row_perm].view(np.uint8)[:, ::4]
        return asc(b.T)

    gate_pkT = pack_u8_T(g_pk, Q)
    up_pkT = pack_u8_T(u_pk, Q)
    down_pkT = pack_u8_T(d_pk, P)
    gsT = asc(g_sc[Q].T.astype(np.float32))
    usT = asc(u_sc[Q].T.astype(np.float32))
    dsT = asc(d_sc[P].T.astype(np.float32))

    w_in_p = asc(w_in[P])
    conv_wT = asc(conv_w.transpose(2, 1, 0))
    w_out_p = asc(w_out[:, P])

    def r128(v):
        return asc(v.reshape(NCH, 128).T.astype(np.float32))

    conv_b_r = r128(conv_b)
    b_dt_r = r128(b_dt)
    d_r = r128(d_in)

    consts = np.zeros((128, 32), dtype=np.float32)
    for ci, v in enumerate(CONST_VALS):
        consts[:, ci] = v
    iota = np.arange(64, dtype=np.int32).reshape(1, 64)
    ones_col = np.full((128, 1), 1.0 / DM, dtype=ml_dtypes.bfloat16)
    ones_row = np.ones((1, 128), dtype=ml_dtypes.bfloat16)
    ones_f32 = np.ones((128, SEG), dtype=np.float32)

    in_maps = []
    for c in range(8):
        pair = c // 2
        parity = c % 2
        batch = pair // 2
        island = pair % 2
        role = 2 * island + parity
        xb = x[batch].T[P]
        if island == 0:
            xs_ = xb[:, parity * PROD:(parity + 1) * PROD] if parity == 0 \
                else np.zeros((DM, PROD), dtype=np.float32)
        else:
            xs_ = xb[:, parity * PROD:(parity + 1) * PROD]
        a_half = a_log[:, parity * 8:(parity + 1) * 8]
        a_log_r = asc(a_half.reshape(NCH, 128, 8).transpose(1, 0, 2)
                      .reshape(128, NCH * 8).astype(np.float32))
        in_maps.append({
            "x_prod": asc(xs_.astype(np.float32)),
            "w_in": w_in_p, "conv_wT": conv_wT, "conv_b_r": conv_b_r,
            "w_x": asc(w_x.astype(np.float32)), "w_dt": asc(w_dt.astype(np.float32)),
            "b_dt_r": b_dt_r, "a_log_r": a_log_r, "d_r": d_r, "w_out": w_out_p,
            "gate_pkT": gate_pkT, "up_pkT": up_pkT, "down_pkT": down_pkT,
            "gsT": gsT, "usT": usT, "dsT": dsT,
            "kconsts": consts, "iota": iota, "ones_col": ones_col,
            "ones_row": ones_row, "ones_f32": ones_f32,
            "role": np.array([[role, parity, island] + [0] * 13, ], dtype=np.int32),
        })

    nc = build_program(debug=debug)
    res = run_bass_kernel_spmd(nc, in_maps, core_ids=list(range(8)))

    if debug:
        globals()["_dbg_res"] = [np.asarray(r["dbg"]) for r in res.results]

    invP = np.argsort(P)
    out = np.empty((2, LSEQ, DM), dtype=np.float32)
    for c in range(8):
        pair = c // 2
        parity = c % 2
        batch = pair // 2
        island = pair % 2
        t0 = 2048 * island + 1024 * parity
        o = np.asarray(res.results[c]["out"])
        out[batch, t0:t0 + OWN, :] = o[invP].T
    return out
